# version 15
# baseline (speedup 1.0000x reference)
import numpy as np
import ml_dtypes
from contextlib import ExitStack

import concourse.bass as bass
import concourse.mybir as mybir
from concourse.bass_utils import run_bass_kernel_spmd

F32 = mybir.dt.float32
BF16 = mybir.dt.bfloat16
I32 = mybir.dt.int32
AF = mybir.ActivationFunctionType
ALU = mybir.AluOpType
AX = mybir.AxisListType

T = 2048
C = 1024
NCH = 8
F = 4096
EPS = 1e-5
NA = 16
BT = 256
NB = T // BT
CT = 512
NC1 = T // CT
NEG = -1e9


class Eng:
    def __init__(self, name):
        self.name = name
        self.count = 0
        self.stream = []
        self.semh = None
        self.ms = None


def build_nc(upto="C2", dbg_scratch=False, c1_tiles=NC1, c1_sub=9):
    nc = bass.Bass("TRN2", target_bir_lowering=False)

    x_d = nc.dram_tensor("x", [T, C], F32, kind="ExternalInput")
    wqkv_d = nc.dram_tensor("wqkv", [C, 3 * C], F32, kind="ExternalInput")
    wout_d = nc.dram_tensor("wout", [C, C], F32, kind="ExternalInput")
    w1_d = nc.dram_tensor("w1", [C, F], F32, kind="ExternalInput")
    w2_d = nc.dram_tensor("w2", [F, C], F32, kind="ExternalInput")
    fc1q_d = nc.dram_tensor("fc1q", [C, F], I32, kind="ExternalInput")
    fc2q_d = nc.dram_tensor("fc2q", [F, C], I32, kind="ExternalInput")
    fc1s_d = nc.dram_tensor("fc1s", [C // 128, F], F32, kind="ExternalInput")
    fc2s_d = nc.dram_tensor("fc2s", [F // 128, C], F32, kind="ExternalInput")
    pp_d = nc.dram_tensor("pp", [128, 160], F32, kind="ExternalInput")
    PPC = {}
    off = 0
    for nm, w in [("ln1_g", 8), ("ln1_b", 8), ("ln2_g", 8), ("ln2_b", 8),
                  ("lnf_g", 8), ("lnf_b", 8), ("b_qkv", 24), ("b_out", 8),
                  ("b1", 32), ("b2", 8), ("fc1b", 32), ("fc2b", 8)]:
        PPC[nm] = off
        off += w
    gb2_d = nc.dram_tensor("gb2", [2, C], F32, kind="ExternalInput")
    bv_d = nc.dram_tensor("bv", [1, C], F32, kind="ExternalInput")
    id_d = nc.dram_tensor("ident", [128, 128], F32, kind="ExternalInput")
    mask_d = nc.dram_tensor("mask", [128, 128], F32, kind="ExternalInput")
    y_d = nc.dram_tensor("y", [T, C], F32, kind="ExternalOutput")

    _sk = "ExternalOutput" if dbg_scratch else "Internal"
    xn32_s = nc.dram_tensor("xn32_s", [NCH, 128, T], F32, kind=_sk)
    xnbf_s = nc.dram_tensor("xnbf_s", [NCH, 128, T], BF16, kind=_sk)
    g_s = nc.dram_tensor("g_s", [NCH, 128, T], BF16, kind=_sk)
    xh_s = nc.dram_tensor("xh_s", [NCH, 128, T], F32, kind=_sk)
    dbgst_d = nc.dram_tensor("dbgst", [128, 96], F32, kind=_sk)
    r1_s = nc.dram_tensor("r1_s", [1, 2048], F32, kind="Internal")

    with ExitStack() as stk:
        WA = 0
        WB = 32768
        PHB = 65536
        PERB = PHB + 14336
        idbf_c = PERB
        onesrow_bf_c = PERB + 128
        onescol_bf_c = PERB + 256
        bvrow_bf_c = PERB + 260
        gb2bf_c = PERB + 1284
        r2bf_c = PERB + 2308
        rhs2bf_c = PERB + 2564
        maskbf_c = PERB + 2880
        r2bf2_c = PERB + 3040
        rhs2bf2_c = PERB + 3296
        NBF = PERB + 3552
        abf = stk.enter_context(nc.sbuf_tensor("abf", [128, NBF], BF16))

        id32_c = 0
        mask_c = 128
        pp_c = 640
        halfb_c = 800
        bqs_c = 808
        onesrow32_c = 816
        r1row_c = 944
        r2row_c = 2992
        rhs2_c = 3248
        rowtmp_c = 3504
        statc_c = 4016
        PHF = 4080
        NF32 = PHF + 6656
        af32 = stk.enter_context(nc.sbuf_tensor("af32", [128, NF32], F32))
        assert NBF * 2 + NF32 * 4 <= 211000, (NBF * 2, NF32 * 4)

        pb = [stk.enter_context(nc.psum_tensor(f"pb{i}", [128, 512], F32))
              for i in range(7)]
        pbbf = stk.enter_context(nc.psum_tensor("pbbf", [128, 1024], BF16))

        sy = Eng("sy"); gp = Eng("gp"); pe = Eng("pe"); dv = Eng("dv"); ac = Eng("ac")
        ALLE = (sy, gp, pe, dv, ac)
        for e in ALLE:
            e.semh = stk.enter_context(nc.semaphore("sem_" + e.name))
        syms = Eng("syms"); gpms = Eng("gpms")
        for e in (syms, gpms):
            e.semh = stk.enter_context(nc.semaphore("sem_" + e.name))
        sy.ms, gp.ms = syms, gpms

        DIS = [False]
        PH_ORDER = ["P0", "AL1", "AL2", "AL3", "AL4", "A1", "A2", "A", "DQ", "B", "C1", "C2"]

        def _stop(name):
            if PH_ORDER.index(upto) <= PH_ORDER.index(name):
                DIS[0] = True

        def emit(e, body, waits=(), inc=1):
            if DIS[0]:
                return e.count
            for (we, wv) in waits:
                assert wv <= we.count, ("wait ahead of emission",
                                        e.name, we.name, wv, we.count)
            if inc:
                e.count += inc
            wl = [(we.semh, wv) for (we, wv) in waits if wv > 0]
            semh = e.semh

            def fn(h, wl=wl, body=body, inc=inc, semh=semh):
                for s, v in wl:
                    h.wait_ge(s, v)
                last = body(h)
                if inc:
                    assert last is not None, "inc step body must return instruction"
                    last.then_inc(semh, inc)
            e.stream.append(fn)
            return e.count

        def dma(e, out_ap, in_ap, waits=()):
            return emit(e, (lambda h, o=out_ap, i=in_ap:
                            h.dma_start(out=o, in_=i)), waits=waits, inc=16)

        def flush(q):
            if DIS[0]:
                return q.ms.count
            q.ms.count += 1
            rv = q.count
            def fn(h, raw=q.semh, rv=rv, msh=q.ms.semh):
                h.wait_ge(raw, rv)
                h.sem_inc(msh, 1)
            q.stream.append(fn)
            return q.ms.count

        def barrier():
            flush(sy)
            flush(gp)
            counts = [(e, e.count) for e in (pe, dv, ac)] + \
                     [(syms, syms.count), (gpms, gpms.count)]
            for e in ALLE:
                ws = [(o, c) for (o, c) in counts if o is not e and c > 0]
                emit(e, lambda h: None, waits=ws, inc=0)

        def bfv(col, n, p0=0, p1=128):
            return abf[p0:p1, col:col + n]

        def fv(col, n, p0=0, p1=128):
            return af32[p0:p1, col:col + n]

        ld_pp = dma(sy, fv(pp_c, 160), pp_d.ap())
        dma(sy, fv(id32_c, 128), id_d.ap())
        ld_bv = dma(sy, fv(PHF, 1024, 0, 1), bv_d.ap())
        ld_gb2 = dma(sy, fv(PHF + 1024, 1024, 0, 2), gb2_d.ap())
        p0_syms = flush(sy)
        dma(gp, bfv(idbf_c, 128), id_d.ap())
        p0_gpms = flush(gp)

        emit(dv, lambda h: h.memset(fv(statc_c, 48), 0.0))
        emit(dv, lambda h: h.memset(fv(statc_c + 63, 1), EPS))
        emit(dv, lambda h: h.memset(fv(onesrow32_c, 128, 0, 1), 1.0))
        emit(dv, lambda h: h.memset(bfv(onesrow_bf_c, 128, 0, 1), 1.0))
        emit(dv, lambda h: h.memset(bfv(onescol_bf_c, 1), 1.0))
        emit(dv, lambda h: h.tensor_scalar_mul(fv(halfb_c, 8),
                                               fv(pp_c + PPC["fc2b"], 8), 0.5),
             waits=[(syms, p0_syms)])
        p0_dv = emit(dv, lambda h: h.tensor_scalar_mul(fv(bqs_c, 8),
                                                       fv(pp_c + PPC["b_qkv"], 8), 0.125))
        p0_bvrow = emit(ac, lambda h: h.copy(bfv(bvrow_bf_c, 1024, 0, 1),
                                             fv(PHF, 1024, 0, 1)),
                        waits=[(syms, p0_syms)])
        p0_ones = p0_dv
        p0_bvmm = emit(pe, lambda h: h.matmul(
            pb[3][:, 0:512], bfv(onesrow_bf_c, 128, 0, 1),
            bfv(bvrow_bf_c, 512, 0, 1), start=True, stop=True),
            waits=[(ac, p0_bvrow), (dv, p0_dv)])
        p0_bvmm = emit(pe, lambda h: h.matmul(
            pb[4][:, 0:512], bfv(onesrow_bf_c, 128, 0, 1),
            bfv(bvrow_bf_c + 512, 512, 0, 1), start=True, stop=True))
        emit(ac, lambda h: h.copy(bfv(bvrow_bf_c, 512), pb[3][:, 0:512]),
             waits=[(pe, p0_bvmm)])
        p0_bvb = emit(ac, lambda h: h.copy(bfv(bvrow_bf_c + 512, 512), pb[4][:, 0:512]))
        p0_ac = emit(ac, lambda h: h.copy(bfv(gb2bf_c, 1024, 0, 2),
                                          fv(PHF + 1024, 1024, 0, 2)),
                     waits=[(syms, p0_syms)])

        _stop("P0")
        xtm_c = [PHF, PHF + 1024]
        sq_c = [PHF + 2048, PHF + 3072]
        xntm_c = PHF + 2048
        stg32_c = [PHF + 3072, PHF + 4096]
        s1p, ssqp, mup, varp, r0p, mbp, u0p, r1p = \
            [PHF + 5120 + 16 * k for k in range(8)]
        stgbf_c = [PHB, PHB + 1024]
        qdq_c = [PHB + 2048, PHB + 6144]
        srow_c = PHB + 10240

        dq_jobs = [("fc1", k) for k in range(8)] + [("fc2", g) for g in range(8)]
        dq_done = [0, 0]
        bank_done = [0, 0]
        srow_pe = 0
        a_ssq_done = [0] * NA
        asq_done = [0] * NA
        for i in range(NA):
            bi = i % 2
            w = []
            if i >= 2:
                w = [(dv, a_ssq_done[i - 2]), (ac, asq_done[i - 2])]
            ld = dma(sy, fv(xtm_c[bi], 1024), x_d.ap()[128 * i:128 * (i + 1), :],
                     waits=w)
            ldms = flush(sy)
            xt = fv(xtm_c[bi], 1024)
            d_s1 = emit(dv, lambda h, xt=xt, i=i: h.reduce_sum(
                fv(s1p + i, 1), xt, AX.X), waits=[(syms, ldms)])
            a_sq = emit(ac, lambda h, xt=xt, bi=bi: h.activation(
                fv(sq_c[bi], 1024), xt, AF.Square),
                waits=[(syms, ldms)] +
                      ([(dv, a_ssq_done[i - 2])] if i >= 2 else []))
            asq_done[i] = a_sq
            a_ssq_done[i] = emit(dv, lambda h, i=i, bi=bi: h.reduce_sum(
                fv(ssqp + i, 1), fv(sq_c[bi], 1024), AX.X),
                waits=[(ac, a_sq)])
            mat, kc = dq_jobs[i]
            q_eng = gp
            if mat == "fc1":
                qsrc = fc1q_d.ap()[128 * kc:128 * (kc + 1), :]
                ssrc = fc1s_d.ap()[kc:kc + 1, :]
                qdst = bfv(qdq_c[bi], 4096)
                sdst = bfv(srow_c, 4096, 0, 1)
                dsts = [WA + kc * 4096 + q5 * 512 for q5 in range(8)]
            else:
                qsrc = fc2q_d.ap()[512 * kc:512 * (kc + 1), :].rearrange(
                    "(b p) n -> p b n", p=128)
                ssrc = fc2s_d.ap().rearrange("(a b) n -> a b n", b=4)[kc:kc + 1]
                qdst = bfv(qdq_c[bi], 4096).rearrange("p (b n) -> p b n", b=4)
                sdst = bfv(srow_c, 4096, 0, 1).rearrange("a (b n) -> a b n", b=4)
                dsts = [WB + (4 * kc + q5 // 2) * 1024 + (q5 % 2) * 512
                        for q5 in range(8)]
            dma(q_eng, qdst, qsrc,
                waits=([(dv, dq_done[bi])] if dq_done[bi] else []))
            dma(q_eng, sdst, ssrc,
                waits=([(pe, srow_pe)] if srow_pe else []))
            dqms = flush(q_eng)
            dq_sem = q_eng.ms
            for q5 in range(8):
                bk = q5 % 2
                pw = [(dq_sem, dqms)]
                if i == 0 and q5 < 2:
                    pw.append((ac, p0_bvb))
                if bank_done[bk]:
                    pw.append((dv, bank_done[bk]))
                srow_pe = emit(pe, lambda h, bk=bk, q5=q5: h.matmul(
                    pb[3 + bk][:, 0:512], bfv(onesrow_bf_c, 128, 0, 1),
                    bfv(srow_c, 4096, 0, 1)[0:1, q5 * 512:(q5 + 1) * 512],
                    start=True, stop=True),
                    waits=pw)
                bank_done[bk] = emit(dv, lambda h, dst=dsts[q5], q5=q5, bi=bi,
                                     bk=bk: h.scalar_tensor_tensor(
                    bfv(dst, 512), bfv(qdq_c[bi] + q5 * 512, 512),
                    -128.0, pb[3 + bk][:, 0:512], ALU.add, ALU.mult),
                    waits=[(pe, srow_pe), (dq_sem, dqms)])
            dq_done[bi] = bank_done[1]

        d_m = emit(dv, lambda h: h.tensor_scalar_mul(fv(mup, 16), fv(s1p, 16),
                                                     1.0 / C))
        d_m = emit(dv, lambda h: h.tensor_scalar_mul(fv(varp, 16), fv(ssqp, 16),
                                                     1.0 / C), waits=[(dv, d_m)])
        d_m = emit(dv, lambda h: h.tensor_mul(fv(s1p, 16), fv(mup, 16),
                                              fv(mup, 16)), waits=[(dv, d_m)])
        d_var = emit(dv, lambda h: h.tensor_sub(fv(varp, 16), fv(varp, 16),
                                                fv(s1p, 16)), waits=[(dv, d_m)])
        a_sd = emit(ac, lambda h: h.activation(fv(s1p, 16), fv(varp, 16), AF.Sqrt,
                                               bias=fv(statc_c + 63, 1), scale=1.0),
                    waits=[(dv, d_var)])
        d_r0 = emit(dv, lambda h: h.reciprocal(fv(r0p, 16), fv(s1p, 16)),
                    waits=[(ac, a_sd)])
        d_mb = emit(dv, lambda h: h.scalar_tensor_tensor(
            fv(mbp, 16), fv(mup, 16), -1.0, fv(r0p, 16), ALU.mult, ALU.mult),
            waits=[(dv, d_r0)])
        d_u = emit(dv, lambda h: h.tensor_mul(fv(u0p, 16), fv(varp, 16),
                                              fv(r0p, 16)), waits=[(dv, d_r0)])
        d_u = emit(dv, lambda h: h.tensor_mul(fv(u0p, 16), fv(u0p, 16),
                                              fv(r0p, 16)), waits=[(dv, d_u)])
        a_u = emit(ac, lambda h: h.activation(fv(u0p, 16), fv(u0p, 16), AF.Sqrt,
                                              bias=fv(statc_c + 63, 1), scale=1.0),
                   waits=[(dv, d_u)])
        d_r1 = emit(dv, lambda h: h.reciprocal(fv(r1p, 16), fv(u0p, 16)),
                    waits=[(ac, a_u)])
        p_r1t = emit(pe, lambda h: h.transpose(pb[2][0:16, 0:128], fv(r1p, 16),
                                               fv(id32_c, 128)),
                     waits=[(dv, d_r1)])
        r1stg = PHF + 5248
        d_r1t = emit(dv, lambda h, r1stg=r1stg: h.tensor_copy(
            fv(r1stg, 128, 0, 16), pb[2][0:16, 0:128]),
            waits=[(pe, p_r1t)])
        dma(gp, r1_s.ap().rearrange("a (i p) -> (a i) p", i=16),
            fv(r1stg, 128, 0, 16), waits=[(dv, d_r1t)])
        r1out_ms = flush(gp)
        ld_r1 = dma(sy, fv(r1row_c, 2048, 0, 1), r1_s.ap(),
                    waits=[(gpms, r1out_ms)])
        r1in_ms = flush(sy)

        a_n_done = [0] * NA
        a_tr_done = [0] * NA
        a_ev_done = [0] * NA
        a_bf_done = [0] * NA
        a_st_done = [0] * NA
        for i in range(NA):
            bi = i % 2
            w = [(dv, d_r1)]
            if i >= 2:
                w.append((ac, a_n_done[i - 2]))
            else:
                w.append((ac, asq_done[NA - 2 + i]))
            ld = dma(sy, fv(xtm_c[bi], 1024), x_d.ap()[128 * i:128 * (i + 1), :],
                     waits=w)
            ldms = flush(sy)
            xt = fv(xtm_c[bi], 1024)
            a_n = emit(ac, lambda h, xt=xt, i=i: h.activation(
                fv(xntm_c, 1024), xt, AF.Identity,
                bias=fv(mbp + i, 1), scale=fv(r0p + i, 1)),
                waits=[(syms, ldms)] +
                      ([(pe, a_tr_done[i - 1])] if i >= 1 else []))
            a_n_done[i] = a_n
            tw = [(ac, a_n)]
            if i >= 1:
                tw += [(dv, a_ev_done[i - 1]), (ac, a_bf_done[i - 1])]
            p_tr = 0
            for c8 in range(8):
                bank = pb[c8 // 4]
                p_tr = emit(pe, lambda h, c8=c8, bank=bank: h.transpose(
                    bank[:, (c8 % 4) * 128:(c8 % 4) * 128 + 128],
                    fv(xntm_c + c8 * 128, 128), fv(id32_c, 128)),
                    waits=(tw if c8 == 0 else []), inc=(1 if c8 == 7 else 0))
            a_tr_done[i] = p_tr
            ew = [(pe, p_tr)]
            if i >= 2:
                ew += [(gpms, a_st_done[i - 2])]
            emit(dv, lambda h, bi=bi: h.tensor_copy(fv(stg32_c[bi], 512),
                                                    pb[0][:, :]), waits=ew)
            d_ev = emit(dv, lambda h, bi=bi: h.tensor_copy(
                fv(stg32_c[bi] + 512, 512), pb[1][:, :]))
            a_ev_done[i] = d_ev
            a_be = emit(ac, lambda h, bi=bi: h.copy(bfv(stgbf_c[bi], 1024),
                                                    fv(stg32_c[bi], 1024)),
                        waits=[(dv, d_ev)])
            a_bf_done[i] = a_be
            for c8 in range(8):
                dma(gp, xn32_s.ap()[c8, :, 128 * i:128 * (i + 1)],
                    fv(stg32_c[bi] + 128 * c8, 128),
                    waits=([(dv, d_ev), (ac, a_be)] if c8 == 0 else []))
                dma(gp, xnbf_s.ap()[c8, :, 128 * i:128 * (i + 1)],
                    bfv(stgbf_c[bi] + 128 * c8, 128))
            a_st_done[i] = flush(gp)

        barrier()
        _stop("A")
        _stop("DQ")

        xnb_c = [PHB, PHB + 2048]
        hfb_c = PHB + 4096
        hid_c = PHB + 6144
        t1_c = PHF
        gst_c = PHF + 512

        b_ln_done = [0] * NB
        b_res_done = [0] * NB
        b_gst_done = [0] * NB
        b_ld = [0] * NB
        prev_r1b_pe = 0
        prev_hid_pe = [0, 0]
        fc1_evac = [0, 0]
        fc2_evac = [0, 0]
        b_hf_done = 0
        for t in range(NB):
            bi = t % 2
            w = []
            if t >= 2:
                w = [(dv, b_res_done[t - 2])]
            for c in range(NCH):
                dma(sy, bfv(xnb_c[bi] + 256 * c, 256),
                    xnbf_s.ap()[c, :, BT * t:BT * (t + 1)],
                    waits=(w if c == 0 else []))
            ldxms = flush(sy)
            r1b = emit(pe, lambda h, t=t: h.matmul(
                pb[0][:, 0:256], fv(onesrow32_c, 128, 0, 1),
                fv(r1row_c + BT * t, 256, 0, 1), start=True, stop=True),
                waits=[(dv, b_ln_done[t - 1])] if t >= 1 else [(dv, p0_dv)])
            dvl = 0
            acl = 0
            for c in range(NCH):
                dvl = emit(dv, lambda h, bi=bi, c=c: h.tensor_mul(
                    fv(t1_c, 256), bfv(xnb_c[bi] + 256 * c, 256), pb[0][:, 0:256]),
                    waits=[(pe, r1b), (syms, ldxms)] if c == 0 else
                          [(ac, acl)])
                acl = emit(ac, lambda h, c=c: h.activation(
                    bfv(hfb_c + 256 * c, 256), fv(t1_c, 256), AF.Identity,
                    bias=fv(pp_c + PPC["lnf_b"] + c, 1),
                    scale=fv(pp_c + PPC["lnf_g"] + c, 1)),
                    waits=[(dv, dvl)] +
                          ([(pe, b_fc1_last)] if t >= 1 and c == 0 else []))
            b_ln_done[t] = dvl
            b_hf_done = acl
            for pr in range(16):
                bank_i = pr % 2
                bank = pb[1 + bank_i]
                for half in range(2):
                    ngc = pr * 2 + half
                    for kc in range(NCH):
                        w2_ = []
                        if kc == 0 and half == 0:
                            w2_ = [(ac, b_hf_done if pr == 0 else b_hf_done)]
                            if fc1_evac[bank_i]:
                                w2_.append((ac, fc1_evac[bank_i]))
                        b_fc1_last = emit(pe, lambda h, bank=bank, half=half,
                                          ngc=ngc, kc=kc: h.matmul(
                            bank[:, half * 256:half * 256 + 256],
                            bfv(WA + kc * 4096 + ngc * 128, 128),
                            bfv(hfb_c + 256 * kc, 256),
                            start=(kc == 0), stop=(kc == NCH - 1)),
                            waits=w2_,
                            inc=(1 if (kc == NCH - 1 and half == 1) else 0))
                ev = 0
                for half in range(2):
                    ngc = pr * 2 + half
                    ev = emit(ac, lambda h, bank=bank, half=half, ngc=ngc: h.activation(
                        bfv(hid_c + 256 * ngc, 256),
                        bank[:, half * 256:half * 256 + 256], AF.Silu,
                        bias=fv(pp_c + PPC["fc1b"] + ngc, 1), scale=1.0),
                        waits=[(pe, b_fc1_last)] if half == 0 else [])
                fc1_evac[bank_i] = ev
            dres = 0
            for c in range(NCH):
                bank_i = c % 2
                bank = pb[3 + bank_i]
                for kc in range(F // 128):
                    w3 = []
                    if kc == 0:
                        w3 = [(ac, fc1_evac[0]), (ac, fc1_evac[1])] if c == 0 else []
                        if fc2_evac[bank_i]:
                            w3.append((dv, fc2_evac[bank_i]))
                    b_fc2_last = emit(pe, lambda h, bank=bank, c=c, kc=kc: h.matmul(
                        bank[:, 0:256],
                        bfv(WB + kc * 1024 + c * 128, 128),
                        bfv(hid_c + 256 * kc, 256),
                        start=(kc == 0), stop=(kc == F // 128 - 1)),
                        waits=w3, inc=(1 if kc == F // 128 - 1 else 0))
                d_u = emit(dv, lambda h, bank=bank, c=c: h.tensor_scalar(
                    fv(t1_c + 256, 256), bank[:, 0:256], 0.5, fv(halfb_c + c, 1),
                    ALU.mult, ALU.add),
                    waits=[(pe, b_fc2_last)] +
                          ([(ac, b_g_ac)] if (t + c) > 0 else []))
                fc2_evac[bank_i] = d_u
                dres = emit(dv, lambda h, bi=bi, c=c: h.tensor_add(
                    fv(t1_c + 256, 256), fv(t1_c + 256, 256),
                    bfv(xnb_c[bi] + 256 * c, 256)), waits=[(dv, d_u)])
                b_g_ac = emit(ac, lambda h, c=c: h.activation(
                    fv(gst_c + 256 * c, 256), fv(t1_c + 256, 256), AF.Silu),
                    waits=[(dv, dres)] +
                          ([(gpms, b_gst_done[t - 1])] if t >= 1 and c == 0 else []))
                dma(gp, g_s.ap()[c, :, BT * t:BT * (t + 1)], fv(gst_c + 256 * c, 256),
                    waits=[(ac, b_g_ac)])
            b_res_done[t] = dres
            b_gst_done[t] = flush(gp)

        barrier()
        _stop("B")

        dma(gp, bfv(maskbf_c, 128), mask_d.ap())
        wqkv_ms = []
        for jb in range(6):
            dma(gp, bfv(WA + jb * 4096, 4096).rearrange("p (c n) -> p c n", c=8),
                wqkv_d.ap()[:, jb * 512:(jb + 1) * 512].rearrange(
                    "(c p) n -> p c n", p=128))
            wqkv_ms.append(flush(gp))
        for kc in range(NCH):
            dma(gp, bfv(WA + 24576 + kc * 1024, 1024),
                wout_d.ap()[128 * kc:128 * (kc + 1), :])
        wout_ms = flush(gp)
        h1_c = WB + 0
        qb_c = WB + 4096
        kk_c = WB + 8192
        val_c = WB + 12800
        vsh_c = WB + 16896
        obs_c = WB + 20992
        pn_c = WB + 25088
        pnT_c = WB + 29184
        assert pnT_c + 2048 <= WB + 32768
        xn32t_c = PHF
        t1b_c = PHF + 4096
        xhst_c = [PHF + 4608, PHF + 5120]
        smx_c = PHF + 5632

        z1 = 0
        for c in range(NCH):
            z1 = emit(dv, lambda h, c=c: h.memset(bfv(kk_c + 576 * c, 64), 0.0))
        z2 = emit(dv, lambda h: h.memset(bfv(vsh_c, 1024, 0, 64), 0.0))

        c1_attn_pe = [0] * NC1
        prev_exps_bank = {}
        prev_ob_evac_last = 0
        prev_proj_evac = [0, 0]
        prev_t1 = 0
        c1_ln_done = [0] * NC1
        c1_xh_st = [0, 0]
        c1_qkv_pe = 0
        for t in range(min(NC1, c1_tiles)):
            for c in range(NCH):
                dma(sy, bfv(PHB + 512 * c, 512),
                    xnbf_s.ap()[c, :, CT * t:CT * (t + 1)],
                    waits=([(dv, c1_ln_done[t - 1])] if c == 0 and t >= 1 else []))
            ldxms = flush(sy)
            for c in range(NCH):
                dma(sy, fv(xn32t_c + 512 * c, 512),
                    xn32_s.ap()[c, :, CT * t:CT * (t + 1)],
                    waits=([(dv, prev_proj_evac[0]),
                            (dv, prev_proj_evac[1])] if c == 0 and t >= 1 else []))
            ld32ms = flush(sy)
            if t >= 1:
                cw = [(pe, c1_attn_pe[t - 1])]
                cr = 0
                for c in range(NCH):
                    cr = emit(dv, lambda h, c=c: h.tensor_copy(
                        bfv(kk_c + 576 * c, 64), bfv(kk_c + 576 * c + 512, 64)),
                        waits=(cw if c == 0 else []))
                cr = emit(dv, lambda h: h.tensor_copy(
                    bfv(vsh_c, 1024, 0, 64), bfv(val_c + 3 * 1024, 1024, 64, 128)),
                    waits=cw)
                carry_done = cr
            else:
                carry_done = z2
            r1w = [(dv, prev_t1)] if prev_t1 else []
            r1b = emit(pe, lambda h, t=t: h.matmul(
                pb[0][:, 0:512], fv(onesrow32_c, 128, 0, 1),
                fv(r1row_c + CT * t, 512, 0, 1), start=True, stop=True),
                waits=r1w)
            dvl = 0
            acl = 0
            acl_c = [0] * NCH
            for c in range(NCH):
                wdv = [(pe, r1b), (syms, ldxms)] if c == 0 else [(ac, acl)]
                dvl = emit(dv, lambda h, c=c: h.tensor_mul(
                    fv(t1b_c, 512), bfv(PHB + 512 * c, 512), pb[0][:, 0:512]),
                    waits=wdv)
                wac = [(dv, dvl)]
                if c == 0 and t >= 1:
                    wac.append((pe, c1_qkv_pe))
                acl = emit(ac, lambda h, c=c: h.activation(
                    bfv(h1_c + 512 * c, 512), fv(t1b_c, 512), AF.Identity,
                    bias=fv(pp_c + PPC["ln1_b"] + c, 1),
                    scale=fv(pp_c + PPC["ln1_g"] + c, 1)),
                    waits=wac)
                acl_c[c] = acl
            c1_ln_done[t] = dvl
            if c1_sub < 2:
                c1_attn_pe[t] = pe.count
                prev_t1 = c1_ln_done[t]
                continue
            vsub_evac = [0, 0]
            for s in range(4):
                bks = (pb[3], pb[4]) if s % 2 == 0 else (pb[5], pb[6])
                for dh in range(2):
                    bank = bks[dh]
                    par = (s % 2) * 2 + dh
                    for kc in range(NCH):
                        w6 = []
                        if s == 0 and dh == 0:
                            w6.append((ac, acl_c[kc]))
                        if kc == 0:
                            if t == 0:
                                w6.append((gpms, wqkv_ms[4 + dh]))
                            if s == 0 and dh == 0:
                                w6.append((dv, carry_done))
                            if s < 2:
                                if t == 0:
                                    if s % 2 == 0:
                                        w6.append((ac, p0_bvb))
                                elif prev_exps_bank.get(par):
                                    w6.append((ac, prev_exps_bank[par]))
                            if s >= 2 and vsub_evac[s % 2]:
                                w6.append((dv, vsub_evac[s % 2]))
                        vmm = emit(pe, lambda h, bank=bank, s=s, dh=dh, kc=kc:
                                   h.matmul(
                            bank[:, 0:512],
                            bfv(h1_c + 512 * kc + 128 * s, 128),
                            bfv(WA + (4 + dh) * 4096 + kc * 512, 512),
                            start=(kc == 0), stop=(kc == NCH - 1)), waits=w6,
                            inc=(1 if kc == NCH - 1 else 0))
                ev = 0
                for dh in range(2):
                    bank = bks[dh]
                    ev = emit(dv, lambda h, bank=bank, s=s, dh=dh: h.tensor_add(
                        bfv(val_c + s * 1024 + dh * 512, 512), bank[:, 0:512],
                        bfv(bvrow_bf_c + dh * 512, 512)),
                        waits=[(pe, vmm)] if dh == 0 else [])
                ev_val = ev
                for dh in range(2):
                    ev = emit(dv, lambda h, s=s, dh=dh: h.tensor_copy(
                        bfv(vsh_c + s * 1024 + dh * 512, 512, 64, 128),
                        bfv(val_c + s * 1024 + dh * 512, 512, 0, 64)),
                        waits=([(dv, ev_val)] if dh == 0 else []))
                    if s < 3:
                        ev = emit(dv, lambda h, s=s, dh=dh: h.tensor_copy(
                            bfv(vsh_c + (s + 1) * 1024 + dh * 512, 512, 0, 64),
                            bfv(val_c + s * 1024 + dh * 512, 512, 64, 128)))
                vsub_evac[s % 2] = ev
            v_done = vsub_evac[1]
            if c1_sub < 3:
                c1_attn_pe[t] = pe.count
                prev_t1 = c1_ln_done[t]
                continue
            groups = [(c, bg) for c in range(NCH) for bg in range(2)]
            smm_g = {}
            exps_bank = dict(prev_exps_bank)
            exp_g = {}
            ts_g = {}
            ptr_g = {}
            pT_g = {}
            omm_g = {}
            q_evac_c = [0] * NCH
            k_evac_c = [0] * NCH
            ob_evac = [0] * NCH

            def emit_q(c):
                for kc in range(NCH):
                    wq = []
                    if kc == 0:
                        if c == 0:
                            if t >= 1:
                                wq.append((dv, prev_proj_evac[0]))
                        else:
                            wq.append((ac, k_evac_c[c - 1]))
                        if t == 0:
                            wq.append((gpms, wqkv_ms[c // 4]))
                    qmm = emit(pe, lambda h, c=c, kc=kc: h.matmul(
                        pb[1][:, 0:512],
                        bfv(WA + (c // 4) * 4096 + kc * 512 + (c % 4) * 128, 128),
                        bfv(h1_c + 512 * kc, 512),
                        start=(kc == 0), stop=(kc == NCH - 1)), waits=wq,
                        inc=(1 if kc == NCH - 1 else 0))
                q_evac_c[c] = emit(ac, lambda h, c=c: h.activation(
                    bfv(qb_c + 512 * c, 512), pb[1][:, 0:512], AF.Identity,
                    bias=fv(bqs_c + c, 1), scale=0.125),
                    waits=[(pe, qmm)])

            def emit_k(c):
                for kc in range(NCH):
                    wk = []
                    if kc == 0:
                        wk.append((ac, q_evac_c[c]))
                        if t == 0:
                            wk.append((gpms, wqkv_ms[2 + c // 4]))
                    kmm = emit(pe, lambda h, c=c, kc=kc: h.matmul(
                        pb[1][:, 0:512],
                        bfv(WA + (2 + c // 4) * 4096 + kc * 512 + (c % 4) * 128, 128),
                        bfv(h1_c + 512 * kc, 512),
                        start=(kc == 0), stop=(kc == NCH - 1)), waits=wk,
                        inc=(1 if kc == NCH - 1 else 0))
                k_evac_c[c] = emit(ac, lambda h, c=c: h.activation(
                    bfv(kk_c + 576 * c + 64, 512), pb[1][:, 0:512], AF.Identity,
                    bias=fv(pp_c + PPC["b_qkv"] + 8 + c, 1), scale=1.0),
                    waits=[(pe, kmm)])

            def stage_a(g):
                c, bg = groups[g]
                par = g % 4
                par8 = g % 8
                sb = pb[3 + par]
                nmx = statc_c + 4 * par
                sm = rowtmp_c + 4 * par8
                smm = 0
                for b4 in range(4):
                    L = bg * 4 + b4
                    w7 = []
                    if b4 == 0:
                        w7.append((ac, k_evac_c[c]))
                        if g < 4:
                            w7.append((dv, v_done))
                        elif exps_bank.get(par):
                            w7.append((ac, exps_bank[par]))
                    for hh in range(2):
                        smm = emit(pe, lambda h, sb=sb, c=c, L=L, b4=b4, hh=hh:
                                   h.matmul(
                            sb[hh * 64:hh * 64 + 64, b4 * 128:b4 * 128 + 128],
                            bfv(qb_c + 512 * c + 64 * L, 64, hh * 64, hh * 64 + 64),
                            bfv(kk_c + 576 * c + 64 * L, 128, hh * 64, hh * 64 + 64),
                            start=True, stop=False),
                            waits=(w7 if hh == 0 else []), inc=0)
                        smm = emit(pe, lambda h, sb=sb, b4=b4, hh=hh: h.matmul(
                            sb[hh * 64:hh * 64 + 64, b4 * 128:b4 * 128 + 128],
                            bfv(idbf_c + hh * 64, 64, hh * 64, hh * 64 + 64),
                            bfv(maskbf_c, 128, hh * 64, hh * 64 + 64),
                            start=False, stop=True),
                            inc=(1 if (b4 == 3 and hh == 1) else 0))
                smm_g[g] = smm
                d_nm = emit(dv, lambda h, sb=sb, nmx=nmx: h.tensor_reduce(
                    fv(nmx, 4), sb[:, :].rearrange("p (g j) -> p g j", g=4),
                    AX.X, ALU.max, negate=True), waits=[(pe, smm)])
                a_e = 0
                for b4 in range(4):
                    w8 = []
                    if b4 == 0:
                        w8 = [(dv, d_nm)]
                        if ptr_g.get(g - 8):
                            w8.append((pe, ptr_g[g - 8]))
                        if ts_g.get(g - 8):
                            w8.append((dv, ts_g[g - 8]))
                    a_e = emit(ac, lambda h, b4=b4, par8=par8, nmx=nmx, sm=sm,
                               sb=sb: h.activation(
                        bfv(pn_c + 512 * par8 + 128 * b4, 128),
                        sb[:, 128 * b4:128 * b4 + 128],
                        AF.Exp, bias=fv(nmx + b4, 1), scale=1.0,
                        accum_out=fv(sm + b4, 1)), waits=w8)
                exp_g[g] = a_e
                exps_bank[par] = a_e

            def stage_b(g):
                c, bg = groups[g]
                par = g % 4
                par8 = g % 8
                sm = rowtmp_c + 4 * par8
                rs = statc_c + 32 + 4 * par
                d_rs = emit(dv, lambda h, sm=sm, rs=rs: h.reciprocal(
                    fv(rs, 4), fv(sm, 4)), waits=[(ac, exp_g[g])])
                d_ts = 0
                for b4 in range(4):
                    d_ts = emit(dv, lambda h, b4=b4, par8=par8, rs=rs:
                                h.tensor_scalar_mul(
                        bfv(pn_c + 512 * par8 + 128 * b4, 128),
                        bfv(pn_c + 512 * par8 + 128 * b4, 128),
                        fv(rs + b4, 1)),
                        waits=([(dv, d_rs)] if b4 == 0 else []))
                ts_g[g] = d_ts
                ptr = 0
                for b4 in range(4):
                    w9 = [(dv, d_ts)] if b4 == 0 else []
                    if b4 == 0 and pT_g.get(g - 2):
                        w9.append((dv, pT_g[g - 2]))
                    ptr = emit(pe, lambda h, par=par, par8=par8, b4=b4: h.transpose(
                        pbbf[:, (par % 2) * 512 + 128 * b4:
                             (par % 2) * 512 + 128 * b4 + 128],
                        bfv(pn_c + 512 * par8 + 128 * b4, 128), bfv(idbf_c, 128)),
                        waits=w9, inc=(1 if b4 == 3 else 0))
                ptr_g[g] = ptr
                wpT = [(pe, ptr)]
                if omm_g.get(g - 4):
                    wpT.append((pe, omm_g[g - 4]))
                d_pT = emit(dv, lambda h, par=par: h.tensor_copy(
                    bfv(pnT_c + 512 * par, 512),
                    pbbf[:, (par % 2) * 512:(par % 2) * 512 + 512]),
                    waits=wpT)
                pT_g[g] = d_pT
                if c1_sub == 41:
                    omm_g[g] = ptr
                    if bg == 1 and c >= 1:
                        ob_evac[c] = ob_evac[c - 1]
                    return
                omm = 0
                obank = pb[2]
                for b4 in range(4):
                    L = bg * 4 + b4
                    vcol = val_c + ((L - 1) // 2) * 1024 if L % 2 == 1 \
                        else vsh_c + (L // 2) * 1024
                    for hh in range(2):
                        w10 = [(dv, d_pT)] if (b4 == 0 and hh == 0) else []
                        if b4 == 0 and hh == 0 and bg == 0:
                            if c >= 1:
                                w10.append((ac, ob_evac[c - 1]))
                            elif t >= 1:
                                w10.append((dv, prev_proj_evac[1]))
                        omm = emit(pe, lambda h, vcol=vcol, c=c, obank=obank,
                                   L=L, hh=hh, par=par: h.matmul(
                            obank[hh * 64:hh * 64 + 64, L * 64:L * 64 + 64],
                            bfv(vcol + c * 128 + hh * 64, 64),
                            bfv(pnT_c + 512 * par + 128 * (L % 4) + hh * 64, 64),
                            start=True, stop=True), waits=w10,
                            inc=(1 if (b4 == 3 and hh == 1) else 0))
                omm_g[g] = omm
                if bg == 1:
                    ob_evac[c] = emit(ac, lambda h, c=c, obank=obank: h.activation(
                        bfv(obs_c + 512 * c, 512), obank[:, 0:512], AF.Identity),
                        waits=[(pe, omm)])

            gi = [0, 0]

            def pump_groups(max_a):
                while gi[0] < max_a:
                    n2 = min(2, max_a - gi[0])
                    for _ in range(n2):
                        stage_a(gi[0])
                        gi[0] += 1
                    if c1_sub == 40:
                        gi[1] = gi[0]
                        continue
                    while gi[1] <= gi[0] - 4 and gi[1] < 16:
                        stage_b(gi[1])
                        gi[1] += 1

            if c1_sub < 4:
                for c in range(NCH):
                    emit_q(c)
                    emit_k(c)
                c1_qkv_pe = pe.count
                c1_attn_pe[t] = pe.count
                prev_t1 = c1_ln_done[t]
                continue
            emit_q(0)
            emit_k(0)
            emit_q(1)
            emit_k(1)
            for c in range(2, NCH):
                emit_q(c)
                pump_groups(2 * (c - 1) - 1)
                emit_k(c)
                if c == NCH - 1:
                    c1_qkv_pe = pe.count
                pump_groups(2 * (c - 1))
            pump_groups(16)
            while gi[1] < 16:
                if c1_sub == 40:
                    gi[1] = 16
                    break
                stage_b(gi[1])
                gi[1] += 1
            c1_attn_pe[t] = pe.count
            if c1_sub < 5 or c1_sub in (40, 41, 42):
                prev_t1 = c1_ln_done[t]
                prev_exps_bank = dict(exps_bank)
                prev_ob_evac_last = ob_evac[NCH - 1]
                continue
            proj_evac = [0, 0]
            for co in range(NCH):
                bank_i = co % 2
                bank = pb[1 + bank_i]
                for kc in range(NCH):
                    w11 = []
                    if kc == 0:
                        w11 = [(ac, ob_evac[NCH - 1])] if co == 0 else []
                        if t == 0 and co == 0:
                            w11.append((gpms, wout_ms))
                        if proj_evac[bank_i]:
                            w11.append((dv, proj_evac[bank_i]))
                    pmm = emit(pe, lambda h, bank=bank, co=co, kc=kc: h.matmul(
                        bank[:, 0:512], bfv(WA + 24576 + kc * 1024 + co * 128, 128),
                        bfv(obs_c + 512 * kc, 512),
                        start=(kc == 0), stop=(kc == NCH - 1)), waits=w11,
                        inc=(1 if kc == NCH - 1 else 0))
                sbuf_i = co % 2
                w12 = [(pe, pmm), (syms, ld32ms)]
                if t >= 1 or co >= 2:
                    w12.append((gpms, c1_xh_st[sbuf_i]))
                d_pj = emit(dv, lambda h, bank=bank, co=co, sbuf_i=sbuf_i:
                            h.scalar_tensor_tensor(
                    fv(xhst_c[sbuf_i], 512), bank[:, 0:512],
                    fv(pp_c + PPC["b_out"] + co, 1), fv(xn32t_c + 512 * co, 512),
                    ALU.add, ALU.add), waits=w12)
                proj_evac[bank_i] = d_pj
                dma(gp, xh_s.ap()[co, :, CT * t:CT * (t + 1)],
                    fv(xhst_c[sbuf_i], 512), waits=[(dv, d_pj)])
                c1_xh_st[sbuf_i] = flush(gp)
            prev_proj_evac = [proj_evac[0], proj_evac[1]]
            prev_exps_bank = dict(exps_bank)
            prev_ob_evac_last = ob_evac[NCH - 1]
            prev_t1 = c1_ln_done[t]

        barrier()
        _stop("C1")

        w1_ms = []
        for jb in range(8):
            dma(gp, bfv(WA + jb * 4096, 4096).rearrange("p (c n) -> p c n", c=8),
                w1_d.ap()[:, jb * 512:(jb + 1) * 512].rearrange(
                    "(c p) n -> p c n", p=128))
            w1_ms.append(flush(gp))
        for g8 in range(8):
            dma(gp, bfv(WB + 4 * g8 * 1024, 4096).rearrange("p (b n) -> p b n", b=4),
                w2_d.ap()[512 * g8:512 * (g8 + 1), :].rearrange(
                    "(b p) n -> p b n", p=128))
        w2_ms = flush(gp)
        xhb_c = PHB
        sqb_c = PHB + 2048
        h2_c = PHB + 4096
        hid2_c = PHB + 6144
        xht_c = PHF
        xtm2_c = PHF + 2048
        ssc_c = PHF + 4096
        gt32_c = PHF + 4352
        sg32_c = PHF + 5376
        assert sg32_c + 1024 <= PHF + 6656

        def gtb(col, n):
            return fv(gt32_c + col // 2, n // 2).bitcast(BF16)

        def sgb(col, n):
            return fv(sg32_c + col // 2, n // 2).bitcast(BF16)

        rw_c = [rowtmp_c, r1row_c]
        r2r_c = [r2row_c, r1row_c + 512]
        rh_c = [rhs2_c, r1row_c + 768]
        r2bf_p = [r2bf_c, r2bf2_c]
        rhs2bf_p = [rhs2bf_c, rhs2bf2_c]

        c2_ldh = [0] * NB
        c2_ldg = [0] * NB
        c2_ldx = [0] * NB
        c2_sg_done = [0] * NB
        c2_store = [0] * NB
        c2_stats_dv = [0] * NB
        c2_sq_smm = [0] * NB
        c2_asq = [0] * NB
        c2_h2_done = [0] * NB
        c2_w1_last = [0] * NB
        c2_tail_pe = [0] * NB
        c2_bmm = [0] * NB
        prev_ab_dv = [0, 0]
        prev_hid2_ac = [0, 0]
        prev_w2_dv = [0, 0]

        def c2_loads(t):
            for c in range(NCH):
                dma(sy, fv(xht_c + 256 * c, 256),
                    xh_s.ap()[c, :, BT * t:BT * (t + 1)],
                    waits=([(dv, c2_sg_done[t - 1])] if c == 0 and t >= 1 else []))
            c2_ldh[t] = flush(sy)
            for c in range(NCH):
                dma(sy, gtb(256 * c, 256),
                    g_s.ap()[c, :, BT * t:BT * (t + 1)],
                    waits=([(dv, c2_sg_done[t - 1])] if c == 0 and t >= 1 else []))
            c2_ldg[t] = flush(sy)
            for s2 in range(2):
                dma(sy, fv(xtm2_c + 1024 * s2, 1024),
                    x_d.ap()[BT * t + 128 * s2:BT * t + 128 * (s2 + 1), :],
                    waits=([(gpms, c2_store[t - 1])] if s2 == 0 and t >= 1 else []))
            c2_ldx[t] = flush(sy)

        def c2_stats(t):
            p = t % 2
            w = [(syms, c2_ldh[t])]
            if t >= 1:
                w += [(pe, c2_sq_smm[t - 1]), (ac, c2_asq[t - 1])]
            d_bf = emit(dv, lambda h: h.tensor_copy(bfv(xhb_c, 2048),
                                                    fv(xht_c, 2048)), waits=w)
            a_sq = emit(ac, lambda h: h.activation(bfv(sqb_c, 2048),
                                                   bfv(xhb_c, 2048), AF.Square),
                        waits=[(dv, d_bf)])
            c2_asq[t] = a_sq
            smm = 0
            for c in range(NCH):
                smm = emit(pe, lambda h, c=c: h.matmul(
                    pb[0][0:1, 0:256], bfv(onescol_bf_c, 1),
                    bfv(xhb_c + 256 * c, 256),
                    start=(c == 0), stop=(c == NCH - 1)),
                    waits=([(dv, d_bf)] +
                           ([(dv, c2_stats_dv[t - 1])] if t >= 1 else []))
                    if c == 0 else [], inc=(1 if c == NCH - 1 else 0))
            for c in range(NCH):
                smm = emit(pe, lambda h, c=c: h.matmul(
                    pb[0][0:1, 256:512], bfv(onescol_bf_c, 1),
                    bfv(sqb_c + 256 * c, 256),
                    start=(c == 0), stop=(c == NCH - 1)),
                    waits=([(ac, a_sq)] if c == 0 else []),
                    inc=(1 if c == NCH - 1 else 0))
            c2_sq_smm[t] = smm
            rw = rw_c[p]
            r2r = r2r_c[p]
            rh = rh_c[p]
            wrm = [(pe, smm)]
            if t >= 2:
                wrm.append((pe, c2_bmm[t - 2]))
            d_p2 = emit(dv, lambda h, rw=rw: h.tensor_scalar_mul(
                fv(rw, 256, 0, 1), pb[0][0:1, 0:256], 1.0 / C), waits=wrm)
            d_p2 = emit(dv, lambda h, rw=rw: h.tensor_scalar_mul(
                fv(rw + 256, 256, 0, 1), pb[0][0:1, 256:512], 1.0 / C),
                waits=[(dv, d_p2)])
            d_p2 = emit(dv, lambda h, rw=rw, r2r=r2r: h.tensor_mul(
                fv(r2r, 256, 0, 1), fv(rw, 256, 0, 1), fv(rw, 256, 0, 1)),
                waits=[(dv, d_p2)])
            d_v2 = emit(dv, lambda h, rw=rw, r2r=r2r: h.tensor_sub(
                fv(rw + 256, 256, 0, 1), fv(rw + 256, 256, 0, 1),
                fv(r2r, 256, 0, 1)), waits=[(dv, d_p2)])
            c2_stats_dv[t] = d_v2
            a_sd = emit(ac, lambda h, rw=rw: h.activation(
                fv(rw + 256, 256, 0, 1), fv(rw + 256, 256, 0, 1), AF.Sqrt,
                bias=fv(statc_c + 63, 1, 0, 1), scale=1.0), waits=[(dv, d_v2)])
            d_r2 = emit(dv, lambda h, rw=rw, r2r=r2r: h.reciprocal(
                fv(r2r, 256, 0, 1), fv(rw + 256, 256, 0, 1)), waits=[(ac, a_sd)])
            d_ms = emit(dv, lambda h, rh=rh: h.memset(fv(rh, 256, 0, 2), -1.0),
                        waits=[(dv, d_r2)])
            d_rh = emit(dv, lambda h, rw=rw, r2r=r2r, rh=rh: h.tensor_mul(
                fv(rh, 256, 0, 1), fv(rw, 256, 0, 1), fv(r2r, 256, 0, 1)),
                waits=[(dv, d_ms)])
            a_r2b = emit(ac, lambda h, p=p, r2r=r2r: h.copy(
                bfv(r2bf_p[p], 256, 0, 1), fv(r2r, 256, 0, 1)),
                waits=[(dv, d_r2)])
            a_rhb = emit(ac, lambda h, p=p, rh=rh: h.copy(
                bfv(rhs2bf_p[p], 256, 0, 2), fv(rh, 256, 0, 2)),
                waits=[(dv, d_rh)])
            d_h2 = 0
            bmm = 0
            for c in range(NCH):
                bank_i = c % 2
                bank = pb[1 + bank_i]
                w14 = [(ac, a_r2b), (ac, a_rhb)] if c == 0 else []
                if c == 0 and t >= 1:
                    w14.append((pe, c2_w1_last[t - 1]))
                if prev_ab_dv[bank_i]:
                    w14.append((dv, prev_ab_dv[bank_i]))
                bmm = emit(pe, lambda h, bank=bank, c=c, p=p: h.matmul(
                    bank[:, 0:256], bfv(gb2bf_c + 128 * c, 128, 0, 1),
                    bfv(r2bf_p[p], 256, 0, 1), start=True, stop=True), waits=w14)
                bmm = emit(pe, lambda h, bank=bank, c=c, p=p: h.matmul(
                    bank[:, 256:512], bfv(gb2bf_c + 128 * c, 128, 0, 2),
                    bfv(rhs2bf_p[p], 256, 0, 2), start=True, stop=True))
                d_tmp = emit(dv, lambda h, bank=bank, c=c: h.tensor_mul(
                    fv(ssc_c, 256), fv(xht_c + 256 * c, 256), bank[:, 0:256]),
                    waits=[(pe, bmm)])
                d_h2 = emit(dv, lambda h, bank=bank, c=c: h.tensor_sub(
                    bfv(h2_c + 256 * c, 256), fv(ssc_c, 256), bank[:, 256:512]),
                    waits=[(dv, d_tmp)])
                prev_ab_dv[bank_i] = d_h2
            c2_bmm[t] = bmm
            c2_h2_done[t] = d_h2

        c2_loads(0)
        c2_stats(0)
        for t in range(NB):
            mm1 = 0
            for pr in range(16):
                bank_i = pr % 2
                bank = pb[3 + bank_i]
                for half in range(2):
                    ngc = pr * 2 + half
                    for kc in range(NCH):
                        w15 = []
                        if kc == 0 and half == 0:
                            w15 = [(dv, c2_h2_done[t])] if pr == 0 else []
                            if t == 0:
                                w15.append((gpms, w1_ms[pr // 2]))
                            if prev_hid2_ac[bank_i]:
                                w15.append((ac, prev_hid2_ac[bank_i]))
                        mm1 = emit(pe, lambda h, bank=bank, half=half, ngc=ngc,
                                   kc=kc: h.matmul(
                            bank[:, half * 256:half * 256 + 256],
                            bfv(WA + (ngc // 4) * 4096 + kc * 512 + (ngc % 4) * 128,
                                128),
                            bfv(h2_c + 256 * kc, 256),
                            start=(kc == 0), stop=(kc == NCH - 1)), waits=w15,
                            inc=(1 if (kc == NCH - 1 and half == 1) else 0))
                ev = 0
                for half in range(2):
                    ngc = pr * 2 + half
                    ev = emit(ac, lambda h, bank=bank, half=half, ngc=ngc:
                              h.activation(
                        bfv(hid2_c + 256 * ngc, 256),
                        bank[:, half * 256:half * 256 + 256], AF.Gelu,
                        bias=fv(pp_c + PPC["b1"] + ngc, 1), scale=1.0),
                        waits=[(pe, mm1)] if half == 0 else [])
                prev_hid2_ac[bank_i] = ev
            c2_w1_last[t] = pe.count
            d_sg = 0
            for c in range(NCH):
                bank_i = c % 2
                bank = pb[5 + bank_i]
                for kc in range(F // 128):
                    w16 = []
                    if kc == 0:
                        w16 = [(ac, prev_hid2_ac[0]), (ac, prev_hid2_ac[1])] \
                            if c == 0 else []
                        if t == 0 and c == 0:
                            w16.append((gpms, w2_ms))
                        if prev_w2_dv[bank_i]:
                            w16.append((dv, prev_w2_dv[bank_i]))
                    mm2 = emit(pe, lambda h, bank=bank, c=c, kc=kc: h.matmul(
                        bank[:, 0:256], bfv(WB + kc * 1024 + c * 128, 128),
                        bfv(hid2_c + 256 * kc, 256),
                        start=(kc == 0), stop=(kc == F // 128 - 1)), waits=w16,
                        inc=(1 if kc == F // 128 - 1 else 0))
                d_s = emit(dv, lambda h, bank=bank, c=c: h.scalar_tensor_tensor(
                    fv(ssc_c, 256), bank[:, 0:256], fv(pp_c + PPC["b2"] + c, 1),
                    fv(xht_c + 256 * c, 256), ALU.add, ALU.add),
                    waits=[(pe, mm2)])
                prev_w2_dv[bank_i] = d_s
                wsg = [(dv, d_s)]
                if c == 0:
                    wsg.append((syms, c2_ldg[t]))
                    if t >= 1:
                        wsg.append((pe, c2_tail_pe[t - 1]))
                d_sg = emit(dv, lambda h, c=c: h.tensor_mul(
                    sgb(256 * c, 256), fv(ssc_c, 256), gtb(256 * c, 256)),
                    waits=wsg)
            c2_sg_done[t] = d_sg
            if t + 1 < NB:
                c2_loads(t + 1)
            dvo = 0
            for s2 in range(2):
                ptr = 0
                for c in range(NCH):
                    w17 = []
                    if c == 0:
                        w17 = [(dv, d_sg)]
                        if t >= 1 or s2 == 1:
                            w17.append((dv, dvo))
                    ptr = emit(pe, lambda h, c=c, s2=s2: h.transpose(
                        pbbf[:, c * 128:c * 128 + 128],
                        sgb(256 * c + 128 * s2, 128), bfv(idbf_c, 128)),
                        waits=w17, inc=(1 if c == NCH - 1 else 0))
                for c in range(NCH):
                    w18 = [(pe, ptr), (syms, c2_ldx[t])] if c == 0 else []
                    dvo = emit(dv, lambda h, c=c, s2=s2: h.tensor_add(
                        fv(xtm2_c + 1024 * s2 + 128 * c, 128),
                        fv(xtm2_c + 1024 * s2 + 128 * c, 128),
                        pbbf[:, c * 128:c * 128 + 128]), waits=w18)
                dma(gp, y_d.ap()[BT * t + 128 * s2:BT * t + 128 * (s2 + 1), :],
                    fv(xtm2_c + 1024 * s2, 1024), waits=[(dv, dvo)])
            c2_tail_pe[t] = pe.count
            c2_store[t] = flush(gp)
            if t + 1 < NB:
                c2_stats(t + 1)

        flush(sy)
        flush(gp)

        with nc.Block() as block:
            @block.sync
            def _(h):
                for fn in sy.stream:
                    fn(h)

            @block.gpsimd
            def _(h):
                for fn in gp.stream:
                    fn(h)

            @block.tensor
            def _(h):
                for fn in pe.stream:
                    fn(h)

            @block.vector
            def _(h):
                for fn in dv.stream:
                    fn(h)

            @block.scalar
            def _(h):
                for fn in ac.stream:
                    fn(h)

    return nc


def _pack_params(ins):
    cols = []
    for nm, key in [("ln1_g", "L0_attn_ln1_g"), ("ln1_b", "L0_attn_ln1_b"),
                    ("ln2_g", "L0_attn_ln2_g"), ("ln2_b", "L0_attn_ln2_b"),
                    ("lnf_g", "L0_ffn_ln_gamma"), ("lnf_b", "L0_ffn_ln_beta"),
                    ("b_qkv", "L0_attn_b_qkv"), ("b_out", "L0_attn_b_out"),
                    ("b1", "L0_attn_b1"), ("b2", "L0_attn_b2"),
                    ("fc1b", "L0_ffn_fc1_bias"), ("fc2b", "L0_ffn_fc2_bias")]:
        v = np.asarray(ins[key], dtype=np.float32)
        cols.append(v.reshape(-1, 128).T)
    return np.ascontiguousarray(np.concatenate(cols, axis=1))


def _mask128():
    i = np.arange(64)
    j = np.arange(128)
    allowed = (j[None, :] < 64) | ((j[None, :] - 64) <= i[:, None])
    m64 = np.where(allowed, 0.0, NEG).astype(np.float32)
    return np.ascontiguousarray(np.concatenate([m64, m64], axis=0))


def kernel(**ins):
    x = np.ascontiguousarray(np.asarray(ins["L0_attn_x"], dtype=np.float32))
    shared = {
        "wqkv": np.ascontiguousarray(np.asarray(ins["L0_attn_w_qkv"], np.float32)),
        "wout": np.ascontiguousarray(np.asarray(ins["L0_attn_w_out"], np.float32)),
        "w1": np.ascontiguousarray(np.asarray(ins["L0_attn_w1"], np.float32)),
        "w2": np.ascontiguousarray(np.asarray(ins["L0_attn_w2"], np.float32)),
        "fc1q": np.ascontiguousarray(np.asarray(ins["L0_ffn_fc1_q"], np.int32)),
        "fc2q": np.ascontiguousarray(np.asarray(ins["L0_ffn_fc2_q"], np.int32)),
        "fc1s": np.ascontiguousarray(np.asarray(ins["L0_ffn_fc1_scales"], np.float32)),
        "fc2s": np.ascontiguousarray(np.asarray(ins["L0_ffn_fc2_scales"], np.float32)),
        "pp": _pack_params(ins),
        "gb2": np.ascontiguousarray(np.stack([
            np.asarray(ins["L0_attn_ln2_g"], np.float32),
            np.asarray(ins["L0_attn_ln2_b"], np.float32)])),
        "bv": np.ascontiguousarray(
            np.asarray(ins["L0_attn_b_qkv"], np.float32)[2048:3072][None, :]),
        "ident": np.eye(128, dtype=np.float32),
        "mask": _mask128(),
    }
    nc = build_nc()
    in_maps = []
    for core in range(8):
        m = dict(shared)
        m["x"] = x[core]
        in_maps.append(m)
    res = run_bass_kernel_spmd(nc, in_maps, core_ids=list(range(8)))
    out = np.stack([np.asarray(r["y"], np.float32) for r in res.results], axis=0)
    return out


if __name__ == "__main__":
    import reference as _r
    ins = _r.setup_inputs()
    out = kernel(**{k: np.asarray(v) for k, v in ins.items()})
    ref = np.asarray(_r.reference(**ins))
    err = np.abs(out - ref)
    denom = np.square(ref).mean()
    print("resvar:", float(np.square(out - ref).mean() / denom),
          "maxabs:", float(err.max()))


# version 16
# speedup vs baseline: 1.0501x; 1.0501x over previous
import numpy as np
import ml_dtypes
from contextlib import ExitStack

import concourse.bass as bass
import concourse.mybir as mybir
from concourse.bass_utils import run_bass_kernel_spmd

F32 = mybir.dt.float32
BF16 = mybir.dt.bfloat16
I32 = mybir.dt.int32
AF = mybir.ActivationFunctionType
ALU = mybir.AluOpType
AX = mybir.AxisListType

T = 2048
C = 1024
NCH = 8
F = 4096
EPS = 1e-5
NA = 16
BT = 256
NB = T // BT
CT = 512
NC1 = T // CT
NEG = -1e9


class Eng:
    def __init__(self, name):
        self.name = name
        self.count = 0
        self.stream = []
        self.semh = None
        self.ms = None


def build_nc(upto="C2", dbg_scratch=False, c1_tiles=NC1, c1_sub=9):
    nc = bass.Bass("TRN2", target_bir_lowering=False)

    x_d = nc.dram_tensor("x", [T, C], F32, kind="ExternalInput")
    wqkv_d = nc.dram_tensor("wqkv", [C, 3 * C], F32, kind="ExternalInput")
    wout_d = nc.dram_tensor("wout", [C, C], F32, kind="ExternalInput")
    w1_d = nc.dram_tensor("w1", [C, F], F32, kind="ExternalInput")
    w2_d = nc.dram_tensor("w2", [F, C], F32, kind="ExternalInput")
    fc1q_d = nc.dram_tensor("fc1q", [C, F], I32, kind="ExternalInput")
    fc2q_d = nc.dram_tensor("fc2q", [F, C], I32, kind="ExternalInput")
    fc1s_d = nc.dram_tensor("fc1s", [C // 128, F], F32, kind="ExternalInput")
    fc2s_d = nc.dram_tensor("fc2s", [F // 128, C], F32, kind="ExternalInput")
    pp_d = nc.dram_tensor("pp", [128, 160], F32, kind="ExternalInput")
    PPC = {}
    off = 0
    for nm, w in [("ln1_g", 8), ("ln1_b", 8), ("ln2_g", 8), ("ln2_b", 8),
                  ("lnf_g", 8), ("lnf_b", 8), ("b_qkv", 24), ("b_out", 8),
                  ("b1", 32), ("b2", 8), ("fc1b", 32), ("fc2b", 8)]:
        PPC[nm] = off
        off += w
    gb2_d = nc.dram_tensor("gb2", [2, C], F32, kind="ExternalInput")
    bv_d = nc.dram_tensor("bv", [1, C], F32, kind="ExternalInput")
    id_d = nc.dram_tensor("ident", [128, 128], F32, kind="ExternalInput")
    mask_d = nc.dram_tensor("mask", [128, 128], F32, kind="ExternalInput")
    y_d = nc.dram_tensor("y", [T, C], F32, kind="ExternalOutput")

    _sk = "ExternalOutput" if dbg_scratch else "Internal"
    xn32_s = nc.dram_tensor("xn32_s", [NA, 128, NCH, 128], F32, kind=_sk)
    xnbf_s = nc.dram_tensor("xnbf_s", [NA, 128, NCH, 128], BF16, kind=_sk)
    g_s = nc.dram_tensor("g_s", [NB, 128, NCH, BT], BF16, kind=_sk)
    xh_s = nc.dram_tensor("xh_s", [NCH, 128, T], F32, kind=_sk)
    dbgst_d = nc.dram_tensor("dbgst", [128, 96], F32, kind=_sk)
    r1_s = nc.dram_tensor("r1_s", [1, 2048], F32, kind="Internal")

    with ExitStack() as stk:
        WA = 0
        WB = 32768
        PHB = 65536
        PERB = PHB + 14336
        idbf_c = PERB
        onesrow_bf_c = PERB + 128
        onescol_bf_c = PERB + 256
        bvrow_bf_c = PERB + 260
        gb2bf_c = PERB + 1284
        r2bf_c = PERB + 2308
        rhs2bf_c = PERB + 2564
        maskbf_c = PERB + 2880
        r2bf2_c = PERB + 3040
        rhs2bf2_c = PERB + 3296
        NBF = PERB + 3552
        abf = stk.enter_context(nc.sbuf_tensor("abf", [128, NBF], BF16))

        id32_c = 0
        mask_c = 128
        pp_c = 640
        halfb_c = 800
        bqs_c = 808
        onesrow32_c = 816
        r1row_c = 944
        r2row_c = 2992
        rhs2_c = 3248
        rowtmp_c = 3504
        statc_c = 4016
        PHF = 4080
        NF32 = PHF + 6656
        af32 = stk.enter_context(nc.sbuf_tensor("af32", [128, NF32], F32))
        assert NBF * 2 + NF32 * 4 <= 211000, (NBF * 2, NF32 * 4)

        pb = [stk.enter_context(nc.psum_tensor(f"pb{i}", [128, 512], F32))
              for i in range(7)]
        pbbf = stk.enter_context(nc.psum_tensor("pbbf", [128, 1024], BF16))

        sy = Eng("sy"); gp = Eng("gp"); pe = Eng("pe"); dv = Eng("dv"); ac = Eng("ac")
        ALLE = (sy, gp, pe, dv, ac)
        for e in ALLE:
            e.semh = stk.enter_context(nc.semaphore("sem_" + e.name))
        syms = Eng("syms"); gpms = Eng("gpms")
        for e in (syms, gpms):
            e.semh = stk.enter_context(nc.semaphore("sem_" + e.name))
        sy.ms, gp.ms = syms, gpms

        DIS = [False]
        PH_ORDER = ["P0", "AL1", "AL2", "AL3", "AL4", "A1", "A2", "A", "DQ", "B", "C1", "C2"]

        def _stop(name):
            if PH_ORDER.index(upto) <= PH_ORDER.index(name):
                DIS[0] = True

        def emit(e, body, waits=(), inc=1):
            if DIS[0]:
                return e.count
            for (we, wv) in waits:
                assert wv <= we.count, ("wait ahead of emission",
                                        e.name, we.name, wv, we.count)
            if inc:
                e.count += inc
            wl = [(we.semh, wv) for (we, wv) in waits if wv > 0]
            semh = e.semh

            def fn(h, wl=wl, body=body, inc=inc, semh=semh):
                for s, v in wl:
                    h.wait_ge(s, v)
                last = body(h)
                if inc:
                    assert last is not None, "inc step body must return instruction"
                    last.then_inc(semh, inc)
            e.stream.append(fn)
            return e.count

        def dma(e, out_ap, in_ap, waits=()):
            return emit(e, (lambda h, o=out_ap, i=in_ap:
                            h.dma_start(out=o, in_=i)), waits=waits, inc=16)

        def flush(q):
            if DIS[0]:
                return q.ms.count
            q.ms.count += 1
            rv = q.count
            def fn(h, raw=q.semh, rv=rv, msh=q.ms.semh):
                h.wait_ge(raw, rv)
                h.sem_inc(msh, 1)
            q.stream.append(fn)
            return q.ms.count

        def barrier():
            flush(sy)
            flush(gp)
            counts = [(e, e.count) for e in (pe, dv, ac)] + \
                     [(syms, syms.count), (gpms, gpms.count)]
            for e in ALLE:
                ws = [(o, c) for (o, c) in counts if o is not e and c > 0]
                emit(e, lambda h: None, waits=ws, inc=0)

        def bfv(col, n, p0=0, p1=128):
            return abf[p0:p1, col:col + n]

        def fv(col, n, p0=0, p1=128):
            return af32[p0:p1, col:col + n]

        ld_pp = dma(sy, fv(pp_c, 160), pp_d.ap())
        dma(sy, fv(id32_c, 128), id_d.ap())
        ld_bv = dma(sy, fv(PHF, 1024, 0, 1), bv_d.ap())
        ld_gb2 = dma(sy, fv(PHF + 1024, 1024, 0, 2), gb2_d.ap())
        p0_syms = flush(sy)
        dma(gp, bfv(idbf_c, 128), id_d.ap())
        p0_gpms = flush(gp)

        emit(dv, lambda h: h.memset(fv(statc_c, 48), 0.0))
        emit(dv, lambda h: h.memset(fv(statc_c + 63, 1), EPS))
        emit(dv, lambda h: h.memset(fv(onesrow32_c, 128, 0, 1), 1.0))
        emit(dv, lambda h: h.memset(bfv(onesrow_bf_c, 128, 0, 1), 1.0))
        emit(dv, lambda h: h.memset(bfv(onescol_bf_c, 1), 1.0))
        emit(dv, lambda h: h.tensor_scalar_mul(fv(halfb_c, 8),
                                               fv(pp_c + PPC["fc2b"], 8), 0.5),
             waits=[(syms, p0_syms)])
        p0_dv = emit(dv, lambda h: h.tensor_scalar_mul(fv(bqs_c, 8),
                                                       fv(pp_c + PPC["b_qkv"], 8), 0.125))
        p0_bvrow = emit(ac, lambda h: h.copy(bfv(bvrow_bf_c, 1024, 0, 1),
                                             fv(PHF, 1024, 0, 1)),
                        waits=[(syms, p0_syms)])
        p0_ones = p0_dv
        p0_bvmm = emit(pe, lambda h: h.matmul(
            pb[3][:, 0:512], bfv(onesrow_bf_c, 128, 0, 1),
            bfv(bvrow_bf_c, 512, 0, 1), start=True, stop=True),
            waits=[(ac, p0_bvrow), (dv, p0_dv)])
        p0_bvmm = emit(pe, lambda h: h.matmul(
            pb[4][:, 0:512], bfv(onesrow_bf_c, 128, 0, 1),
            bfv(bvrow_bf_c + 512, 512, 0, 1), start=True, stop=True))
        emit(ac, lambda h: h.copy(bfv(bvrow_bf_c, 512), pb[3][:, 0:512]),
             waits=[(pe, p0_bvmm)])
        p0_bvb = emit(ac, lambda h: h.copy(bfv(bvrow_bf_c + 512, 512), pb[4][:, 0:512]))
        p0_ac = emit(ac, lambda h: h.copy(bfv(gb2bf_c, 1024, 0, 2),
                                          fv(PHF + 1024, 1024, 0, 2)),
                     waits=[(syms, p0_syms)])

        _stop("P0")
        xtm_c = [PHF, PHF + 1024]
        sq_c = [PHF + 2048, PHF + 3072]
        xntm_c = PHF + 2048
        stg32_c = [PHF + 3072, PHF + 4096]
        s1p, ssqp, mup, varp, r0p, mbp, u0p, r1p = \
            [PHF + 5120 + 16 * k for k in range(8)]
        stgbf_c = [PHB, PHB + 1024]
        qdq_c = [PHB + 2048, PHB + 6144]
        srow_c = PHB + 10240

        dq_jobs = [("fc1", k) for k in range(8)] + [("fc2", g) for g in range(8)]
        dq_done = [0, 0]
        bank_done = [0, 0]
        srow_pe = 0
        a_ssq_done = [0] * NA
        asq_done = [0] * NA
        for i in range(NA):
            bi = i % 2
            w = []
            if i >= 2:
                w = [(dv, a_ssq_done[i - 2]), (ac, asq_done[i - 2])]
            ld = dma(sy, fv(xtm_c[bi], 1024), x_d.ap()[128 * i:128 * (i + 1), :],
                     waits=w)
            ldms = flush(sy)
            xt = fv(xtm_c[bi], 1024)
            d_s1 = emit(dv, lambda h, xt=xt, i=i: h.reduce_sum(
                fv(s1p + i, 1), xt, AX.X), waits=[(syms, ldms)])
            a_sq = emit(ac, lambda h, xt=xt, bi=bi: h.activation(
                fv(sq_c[bi], 1024), xt, AF.Square),
                waits=[(syms, ldms)] +
                      ([(dv, a_ssq_done[i - 2])] if i >= 2 else []))
            asq_done[i] = a_sq
            a_ssq_done[i] = emit(dv, lambda h, i=i, bi=bi: h.reduce_sum(
                fv(ssqp + i, 1), fv(sq_c[bi], 1024), AX.X),
                waits=[(ac, a_sq)])
            mat, kc = dq_jobs[i]
            q_eng = gp
            if mat == "fc1":
                qsrc = fc1q_d.ap()[128 * kc:128 * (kc + 1), :]
                ssrc = fc1s_d.ap()[kc:kc + 1, :]
                qdst = bfv(qdq_c[bi], 4096)
                sdst = bfv(srow_c, 4096, 0, 1)
                dsts = [WA + kc * 4096 + q5 * 512 for q5 in range(8)]
            else:
                qsrc = fc2q_d.ap()[512 * kc:512 * (kc + 1), :].rearrange(
                    "(b p) n -> p b n", p=128)
                ssrc = fc2s_d.ap().rearrange("(a b) n -> a b n", b=4)[kc:kc + 1]
                qdst = bfv(qdq_c[bi], 4096).rearrange("p (b n) -> p b n", b=4)
                sdst = bfv(srow_c, 4096, 0, 1).rearrange("a (b n) -> a b n", b=4)
                dsts = [WB + (4 * kc + q5 // 2) * 1024 + (q5 % 2) * 512
                        for q5 in range(8)]
            dma(q_eng, qdst, qsrc,
                waits=([(dv, dq_done[bi])] if dq_done[bi] else []))
            dma(q_eng, sdst, ssrc,
                waits=([(pe, srow_pe)] if srow_pe else []))
            dqms = flush(q_eng)
            dq_sem = q_eng.ms
            for q5 in range(8):
                bk = q5 % 2
                pw = [(dq_sem, dqms)]
                if i == 0 and q5 < 2:
                    pw.append((ac, p0_bvb))
                if bank_done[bk]:
                    pw.append((dv, bank_done[bk]))
                srow_pe = emit(pe, lambda h, bk=bk, q5=q5: h.matmul(
                    pb[3 + bk][:, 0:512], bfv(onesrow_bf_c, 128, 0, 1),
                    bfv(srow_c, 4096, 0, 1)[0:1, q5 * 512:(q5 + 1) * 512],
                    start=True, stop=True),
                    waits=pw)
                bank_done[bk] = emit(dv, lambda h, dst=dsts[q5], q5=q5, bi=bi,
                                     bk=bk: h.scalar_tensor_tensor(
                    bfv(dst, 512), bfv(qdq_c[bi] + q5 * 512, 512),
                    -128.0, pb[3 + bk][:, 0:512], ALU.add, ALU.mult),
                    waits=[(pe, srow_pe), (dq_sem, dqms)])
            dq_done[bi] = bank_done[1]

        d_m = emit(dv, lambda h: h.tensor_scalar_mul(fv(mup, 16), fv(s1p, 16),
                                                     1.0 / C))
        d_m = emit(dv, lambda h: h.tensor_scalar_mul(fv(varp, 16), fv(ssqp, 16),
                                                     1.0 / C), waits=[(dv, d_m)])
        d_m = emit(dv, lambda h: h.tensor_mul(fv(s1p, 16), fv(mup, 16),
                                              fv(mup, 16)), waits=[(dv, d_m)])
        d_var = emit(dv, lambda h: h.tensor_sub(fv(varp, 16), fv(varp, 16),
                                                fv(s1p, 16)), waits=[(dv, d_m)])
        a_sd = emit(ac, lambda h: h.activation(fv(s1p, 16), fv(varp, 16), AF.Sqrt,
                                               bias=fv(statc_c + 63, 1), scale=1.0),
                    waits=[(dv, d_var)])
        d_r0 = emit(dv, lambda h: h.reciprocal(fv(r0p, 16), fv(s1p, 16)),
                    waits=[(ac, a_sd)])
        d_mb = emit(dv, lambda h: h.scalar_tensor_tensor(
            fv(mbp, 16), fv(mup, 16), -1.0, fv(r0p, 16), ALU.mult, ALU.mult),
            waits=[(dv, d_r0)])
        d_u = emit(dv, lambda h: h.tensor_mul(fv(u0p, 16), fv(varp, 16),
                                              fv(r0p, 16)), waits=[(dv, d_r0)])
        d_u = emit(dv, lambda h: h.tensor_mul(fv(u0p, 16), fv(u0p, 16),
                                              fv(r0p, 16)), waits=[(dv, d_u)])
        a_u = emit(ac, lambda h: h.activation(fv(u0p, 16), fv(u0p, 16), AF.Sqrt,
                                              bias=fv(statc_c + 63, 1), scale=1.0),
                   waits=[(dv, d_u)])
        d_r1 = emit(dv, lambda h: h.reciprocal(fv(r1p, 16), fv(u0p, 16)),
                    waits=[(ac, a_u)])
        p_r1t = emit(pe, lambda h: h.transpose(pb[2][0:16, 0:128], fv(r1p, 16),
                                               fv(id32_c, 128)),
                     waits=[(dv, d_r1)])
        r1stg = PHF + 5248
        d_r1t = emit(dv, lambda h, r1stg=r1stg: h.tensor_copy(
            fv(r1stg, 128, 0, 16), pb[2][0:16, 0:128]),
            waits=[(pe, p_r1t)])
        dma(gp, r1_s.ap().rearrange("a (i p) -> (a i) p", i=16),
            fv(r1stg, 128, 0, 16), waits=[(dv, d_r1t)])
        r1out_ms = flush(gp)
        ld_r1 = dma(sy, fv(r1row_c, 2048, 0, 1), r1_s.ap(),
                    waits=[(gpms, r1out_ms)])
        r1in_ms = flush(sy)

        a_n_done = [0] * NA
        a_tr_done = [0] * NA
        a_ev_done = [0] * NA
        a_bf_done = [0] * NA
        a_st_done = [0] * NA
        for i in range(NA):
            bi = i % 2
            w = [(dv, d_r1)]
            if i >= 2:
                w.append((ac, a_n_done[i - 2]))
            else:
                w.append((ac, asq_done[NA - 2 + i]))
            ld = dma(sy, fv(xtm_c[bi], 1024), x_d.ap()[128 * i:128 * (i + 1), :],
                     waits=w)
            ldms = flush(sy)
            xt = fv(xtm_c[bi], 1024)
            a_n = emit(ac, lambda h, xt=xt, i=i: h.activation(
                fv(xntm_c, 1024), xt, AF.Identity,
                bias=fv(mbp + i, 1), scale=fv(r0p + i, 1)),
                waits=[(syms, ldms)] +
                      ([(pe, a_tr_done[i - 1])] if i >= 1 else []))
            a_n_done[i] = a_n
            tw = [(ac, a_n)]
            if i >= 1:
                tw += [(dv, a_ev_done[i - 1]), (ac, a_bf_done[i - 1])]
            p_tr = 0
            for c8 in range(8):
                bank = pb[c8 // 4]
                p_tr = emit(pe, lambda h, c8=c8, bank=bank: h.transpose(
                    bank[:, (c8 % 4) * 128:(c8 % 4) * 128 + 128],
                    fv(xntm_c + c8 * 128, 128), fv(id32_c, 128)),
                    waits=(tw if c8 == 0 else []), inc=(1 if c8 == 7 else 0))
            a_tr_done[i] = p_tr
            ew = [(pe, p_tr)]
            if i >= 2:
                ew += [(gpms, a_st_done[i - 2])]
            emit(dv, lambda h, bi=bi: h.tensor_copy(fv(stg32_c[bi], 512),
                                                    pb[0][:, :]), waits=ew)
            d_ev = emit(dv, lambda h, bi=bi: h.tensor_copy(
                fv(stg32_c[bi] + 512, 512), pb[1][:, :]))
            a_ev_done[i] = d_ev
            a_be = emit(ac, lambda h, bi=bi: h.copy(bfv(stgbf_c[bi], 1024),
                                                    fv(stg32_c[bi], 1024)),
                        waits=[(dv, d_ev)])
            a_bf_done[i] = a_be
            dma(gp, xn32_s.ap()[i].rearrange("p c n -> p (c n)"),
                fv(stg32_c[bi], 1024), waits=[(dv, d_ev), (ac, a_be)])
            dma(gp, xnbf_s.ap()[i].rearrange("p c n -> p (c n)"),
                bfv(stgbf_c[bi], 1024))
            a_st_done[i] = flush(gp)

        barrier()
        _stop("A")
        _stop("DQ")

        xnb_c = [PHB, PHB + 2048]
        hfb_c = PHB + 4096
        hid_c = PHB + 6144
        t1_c = PHF
        gst_c = PHF + 512

        b_ln_done = [0] * NB
        b_res_done = [0] * NB
        b_gst_done = [0] * NB
        b_ld = [0] * NB
        prev_r1b_pe = 0
        prev_hid_pe = [0, 0]
        fc1_evac = [0, 0]
        fc2_evac = [0, 0]
        b_hf_done = 0
        for t in range(NB):
            bi = t % 2
            w = []
            if t >= 2:
                w = [(dv, b_res_done[t - 2])]
            for c in range(NCH):
                dma(sy, bfv(xnb_c[bi] + 256 * c, 256).rearrange(
                        "p (k n) -> p k n", k=2),
                    xnbf_s.ap()[2 * t:2 * t + 2, :, c, :].rearrange(
                        "k p n -> p k n"),
                    waits=(w if c == 0 else []))
            ldxms = flush(sy)
            r1b = emit(pe, lambda h, t=t: h.matmul(
                pb[0][:, 0:256], fv(onesrow32_c, 128, 0, 1),
                fv(r1row_c + BT * t, 256, 0, 1), start=True, stop=True),
                waits=[(dv, b_ln_done[t - 1])] if t >= 1 else [(dv, p0_dv)])
            dvl = 0
            acl = 0
            for c in range(NCH):
                dvl = emit(dv, lambda h, bi=bi, c=c: h.tensor_mul(
                    fv(t1_c, 256), bfv(xnb_c[bi] + 256 * c, 256), pb[0][:, 0:256]),
                    waits=[(pe, r1b), (syms, ldxms)] if c == 0 else
                          [(ac, acl)])
                acl = emit(ac, lambda h, c=c: h.activation(
                    bfv(hfb_c + 256 * c, 256), fv(t1_c, 256), AF.Identity,
                    bias=fv(pp_c + PPC["lnf_b"] + c, 1),
                    scale=fv(pp_c + PPC["lnf_g"] + c, 1)),
                    waits=[(dv, dvl)] +
                          ([(pe, b_fc1_last)] if t >= 1 and c == 0 else []))
            b_ln_done[t] = dvl
            b_hf_done = acl
            for pr in range(16):
                bank_i = pr % 2
                bank = pb[1 + bank_i]
                for half in range(2):
                    ngc = pr * 2 + half
                    for kc in range(NCH):
                        w2_ = []
                        if kc == 0 and half == 0:
                            w2_ = [(ac, b_hf_done if pr == 0 else b_hf_done)]
                            if fc1_evac[bank_i]:
                                w2_.append((ac, fc1_evac[bank_i]))
                        b_fc1_last = emit(pe, lambda h, bank=bank, half=half,
                                          ngc=ngc, kc=kc: h.matmul(
                            bank[:, half * 256:half * 256 + 256],
                            bfv(WA + kc * 4096 + ngc * 128, 128),
                            bfv(hfb_c + 256 * kc, 256),
                            start=(kc == 0), stop=(kc == NCH - 1)),
                            waits=w2_,
                            inc=(1 if (kc == NCH - 1 and half == 1) else 0))
                ev = 0
                for half in range(2):
                    ngc = pr * 2 + half
                    ev = emit(ac, lambda h, bank=bank, half=half, ngc=ngc: h.activation(
                        bfv(hid_c + 256 * ngc, 256),
                        bank[:, half * 256:half * 256 + 256], AF.Silu,
                        bias=fv(pp_c + PPC["fc1b"] + ngc, 1), scale=1.0),
                        waits=[(pe, b_fc1_last)] if half == 0 else [])
                fc1_evac[bank_i] = ev
            dres = 0
            for c in range(NCH):
                bank_i = c % 2
                bank = pb[3 + bank_i]
                for kc in range(F // 128):
                    w3 = []
                    if kc == 0:
                        w3 = [(ac, fc1_evac[0]), (ac, fc1_evac[1])] if c == 0 else []
                        if fc2_evac[bank_i]:
                            w3.append((dv, fc2_evac[bank_i]))
                    b_fc2_last = emit(pe, lambda h, bank=bank, c=c, kc=kc: h.matmul(
                        bank[:, 0:256],
                        bfv(WB + kc * 1024 + c * 128, 128),
                        bfv(hid_c + 256 * kc, 256),
                        start=(kc == 0), stop=(kc == F // 128 - 1)),
                        waits=w3, inc=(1 if kc == F // 128 - 1 else 0))
                d_u = emit(dv, lambda h, bank=bank, c=c: h.tensor_scalar(
                    fv(t1_c + 256, 256), bank[:, 0:256], 0.5, fv(halfb_c + c, 1),
                    ALU.mult, ALU.add),
                    waits=[(pe, b_fc2_last)] +
                          ([(ac, b_g_ac)] if (t + c) > 0 else []))
                fc2_evac[bank_i] = d_u
                dres = emit(dv, lambda h, bi=bi, c=c: h.tensor_add(
                    fv(t1_c + 256, 256), fv(t1_c + 256, 256),
                    bfv(xnb_c[bi] + 256 * c, 256)), waits=[(dv, d_u)])
                b_g_ac = emit(ac, lambda h, c=c: h.activation(
                    fv(gst_c + 256 * c, 256), fv(t1_c + 256, 256), AF.Silu),
                    waits=[(dv, dres)] +
                          ([(gpms, b_gst_done[t - 1])] if t >= 1 and c == 0 else []))
            dma(gp, g_s.ap()[t].rearrange("p c n -> p (c n)"), fv(gst_c, 2048),
                waits=[(ac, b_g_ac)])
            b_res_done[t] = dres
            b_gst_done[t] = flush(gp)

        barrier()
        _stop("B")

        dma(gp, bfv(maskbf_c, 128), mask_d.ap())
        wqkv_ms = []
        for jb in range(6):
            dma(gp, bfv(WA + jb * 4096, 4096).rearrange("p (c n) -> p c n", c=8),
                wqkv_d.ap()[:, jb * 512:(jb + 1) * 512].rearrange(
                    "(c p) n -> p c n", p=128))
            wqkv_ms.append(flush(gp))
        for kc in range(NCH):
            dma(gp, bfv(WA + 24576 + kc * 1024, 1024),
                wout_d.ap()[128 * kc:128 * (kc + 1), :])
        wout_ms = flush(gp)
        h1_c = WB + 0
        qb_c = WB + 4096
        kk_c = WB + 8192
        val_c = WB + 12800
        vsh_c = WB + 16896
        obs_c = WB + 20992
        pn_c = WB + 25088
        pnT_c = WB + 29184
        assert pnT_c + 2048 <= WB + 32768
        xn32t_c = PHF
        t1b_c = PHF + 4096
        xhst_c = [PHF + 4608, PHF + 5120]
        smx_c = PHF + 5632

        z1 = 0
        for c in range(NCH):
            z1 = emit(dv, lambda h, c=c: h.memset(bfv(kk_c + 576 * c, 64), 0.0))
        z2 = emit(dv, lambda h: h.memset(bfv(vsh_c, 1024, 0, 64), 0.0))

        c1_attn_pe = [0] * NC1
        prev_exps_bank = {}
        prev_ob_evac_last = 0
        prev_proj_evac = [0, 0]
        prev_t1 = 0
        c1_ln_done = [0] * NC1
        c1_xh_st = [0, 0]
        c1_qkv_pe = 0
        for t in range(min(NC1, c1_tiles)):
            for c in range(NCH):
                dma(sy, bfv(PHB + 512 * c, 512).rearrange(
                        "p (k n) -> p k n", k=4),
                    xnbf_s.ap()[4 * t:4 * t + 4, :, c, :].rearrange(
                        "k p n -> p k n"),
                    waits=([(dv, c1_ln_done[t - 1])] if c == 0 and t >= 1 else []))
            ldxms = flush(sy)
            for c in range(NCH):
                dma(sy, fv(xn32t_c + 512 * c, 512).rearrange(
                        "p (k n) -> p k n", k=4),
                    xn32_s.ap()[4 * t:4 * t + 4, :, c, :].rearrange(
                        "k p n -> p k n"),
                    waits=([(dv, prev_proj_evac[0]),
                            (dv, prev_proj_evac[1])] if c == 0 and t >= 1 else []))
            ld32ms = flush(sy)
            if t >= 1:
                cw = [(pe, c1_attn_pe[t - 1])]
                cr = 0
                for c in range(NCH):
                    cr = emit(dv, lambda h, c=c: h.tensor_copy(
                        bfv(kk_c + 576 * c, 64), bfv(kk_c + 576 * c + 512, 64)),
                        waits=(cw if c == 0 else []))
                cr = emit(dv, lambda h: h.tensor_copy(
                    bfv(vsh_c, 1024, 0, 64), bfv(val_c + 3 * 1024, 1024, 64, 128)),
                    waits=cw)
                carry_done = cr
            else:
                carry_done = z2
            r1w = [(dv, prev_t1)] if prev_t1 else []
            r1b = emit(pe, lambda h, t=t: h.matmul(
                pb[0][:, 0:512], fv(onesrow32_c, 128, 0, 1),
                fv(r1row_c + CT * t, 512, 0, 1), start=True, stop=True),
                waits=r1w)
            dvl = 0
            acl = 0
            acl_c = [0] * NCH
            for c in range(NCH):
                wdv = [(pe, r1b), (syms, ldxms)] if c == 0 else [(ac, acl)]
                dvl = emit(dv, lambda h, c=c: h.tensor_mul(
                    fv(t1b_c, 512), bfv(PHB + 512 * c, 512), pb[0][:, 0:512]),
                    waits=wdv)
                wac = [(dv, dvl)]
                if c == 0 and t >= 1:
                    wac.append((pe, c1_qkv_pe))
                acl = emit(ac, lambda h, c=c: h.activation(
                    bfv(h1_c + 512 * c, 512), fv(t1b_c, 512), AF.Identity,
                    bias=fv(pp_c + PPC["ln1_b"] + c, 1),
                    scale=fv(pp_c + PPC["ln1_g"] + c, 1)),
                    waits=wac)
                acl_c[c] = acl
            c1_ln_done[t] = dvl
            if c1_sub < 2:
                c1_attn_pe[t] = pe.count
                prev_t1 = c1_ln_done[t]
                continue
            vsub_evac = [0, 0]
            for s in range(4):
                bks = (pb[3], pb[4]) if s % 2 == 0 else (pb[5], pb[6])
                for dh in range(2):
                    bank = bks[dh]
                    par = (s % 2) * 2 + dh
                    for kc in range(NCH):
                        w6 = []
                        if s == 0 and dh == 0:
                            w6.append((ac, acl_c[kc]))
                        if kc == 0:
                            if t == 0:
                                w6.append((gpms, wqkv_ms[4 + dh]))
                            if s == 0 and dh == 0:
                                w6.append((dv, carry_done))
                            if s < 2:
                                if t == 0:
                                    if s % 2 == 0:
                                        w6.append((ac, p0_bvb))
                                elif prev_exps_bank.get(par):
                                    w6.append((ac, prev_exps_bank[par]))
                            if s >= 2 and vsub_evac[s % 2]:
                                w6.append((dv, vsub_evac[s % 2]))
                        vmm = emit(pe, lambda h, bank=bank, s=s, dh=dh, kc=kc:
                                   h.matmul(
                            bank[:, 0:512],
                            bfv(h1_c + 512 * kc + 128 * s, 128),
                            bfv(WA + (4 + dh) * 4096 + kc * 512, 512),
                            start=(kc == 0), stop=(kc == NCH - 1)), waits=w6,
                            inc=(1 if kc == NCH - 1 else 0))
                ev = 0
                for dh in range(2):
                    bank = bks[dh]
                    ev = emit(dv, lambda h, bank=bank, s=s, dh=dh: h.tensor_add(
                        bfv(val_c + s * 1024 + dh * 512, 512), bank[:, 0:512],
                        bfv(bvrow_bf_c + dh * 512, 512)),
                        waits=[(pe, vmm)] if dh == 0 else [])
                ev_val = ev
                for dh in range(2):
                    ev = emit(dv, lambda h, s=s, dh=dh: h.tensor_copy(
                        bfv(vsh_c + s * 1024 + dh * 512, 512, 64, 128),
                        bfv(val_c + s * 1024 + dh * 512, 512, 0, 64)),
                        waits=([(dv, ev_val)] if dh == 0 else []))
                    if s < 3:
                        ev = emit(dv, lambda h, s=s, dh=dh: h.tensor_copy(
                            bfv(vsh_c + (s + 1) * 1024 + dh * 512, 512, 0, 64),
                            bfv(val_c + s * 1024 + dh * 512, 512, 64, 128)))
                vsub_evac[s % 2] = ev
            v_done = vsub_evac[1]
            if c1_sub < 3:
                c1_attn_pe[t] = pe.count
                prev_t1 = c1_ln_done[t]
                continue
            groups = [(c, bg) for c in range(NCH) for bg in range(2)]
            smm_g = {}
            exps_bank = dict(prev_exps_bank)
            exp_g = {}
            ts_g = {}
            ptr_g = {}
            pT_g = {}
            omm_g = {}
            q_evac_c = [0] * NCH
            k_evac_c = [0] * NCH
            ob_evac = [0] * NCH

            def emit_q(c):
                for kc in range(NCH):
                    wq = []
                    if kc == 0:
                        if c == 0:
                            if t >= 1:
                                wq.append((dv, prev_proj_evac[0]))
                        else:
                            wq.append((ac, k_evac_c[c - 1]))
                        if t == 0:
                            wq.append((gpms, wqkv_ms[c // 4]))
                    qmm = emit(pe, lambda h, c=c, kc=kc: h.matmul(
                        pb[1][:, 0:512],
                        bfv(WA + (c // 4) * 4096 + kc * 512 + (c % 4) * 128, 128),
                        bfv(h1_c + 512 * kc, 512),
                        start=(kc == 0), stop=(kc == NCH - 1)), waits=wq,
                        inc=(1 if kc == NCH - 1 else 0))
                q_evac_c[c] = emit(ac, lambda h, c=c: h.activation(
                    bfv(qb_c + 512 * c, 512), pb[1][:, 0:512], AF.Identity,
                    bias=fv(bqs_c + c, 1), scale=0.125),
                    waits=[(pe, qmm)])

            def emit_k(c):
                for kc in range(NCH):
                    wk = []
                    if kc == 0:
                        wk.append((ac, q_evac_c[c]))
                        if t == 0:
                            wk.append((gpms, wqkv_ms[2 + c // 4]))
                    kmm = emit(pe, lambda h, c=c, kc=kc: h.matmul(
                        pb[1][:, 0:512],
                        bfv(WA + (2 + c // 4) * 4096 + kc * 512 + (c % 4) * 128, 128),
                        bfv(h1_c + 512 * kc, 512),
                        start=(kc == 0), stop=(kc == NCH - 1)), waits=wk,
                        inc=(1 if kc == NCH - 1 else 0))
                k_evac_c[c] = emit(ac, lambda h, c=c: h.activation(
                    bfv(kk_c + 576 * c + 64, 512), pb[1][:, 0:512], AF.Identity,
                    bias=fv(pp_c + PPC["b_qkv"] + 8 + c, 1), scale=1.0),
                    waits=[(pe, kmm)])

            def stage_a(g):
                c, bg = groups[g]
                par = g % 4
                par8 = g % 8
                sb = pb[3 + par]
                nmx = statc_c + 4 * par
                sm = rowtmp_c + 4 * par8
                smm = 0
                for b4 in range(4):
                    L = bg * 4 + b4
                    w7 = []
                    if b4 == 0:
                        w7.append((ac, k_evac_c[c]))
                        if g < 4:
                            w7.append((dv, v_done))
                        elif exps_bank.get(par):
                            w7.append((ac, exps_bank[par]))
                    for hh in range(2):
                        smm = emit(pe, lambda h, sb=sb, c=c, L=L, b4=b4, hh=hh:
                                   h.matmul(
                            sb[hh * 64:hh * 64 + 64, b4 * 128:b4 * 128 + 128],
                            bfv(qb_c + 512 * c + 64 * L, 64, hh * 64, hh * 64 + 64),
                            bfv(kk_c + 576 * c + 64 * L, 128, hh * 64, hh * 64 + 64),
                            start=True, stop=False),
                            waits=(w7 if hh == 0 else []), inc=0)
                        smm = emit(pe, lambda h, sb=sb, b4=b4, hh=hh: h.matmul(
                            sb[hh * 64:hh * 64 + 64, b4 * 128:b4 * 128 + 128],
                            bfv(idbf_c + hh * 64, 64, hh * 64, hh * 64 + 64),
                            bfv(maskbf_c, 128, hh * 64, hh * 64 + 64),
                            start=False, stop=True),
                            inc=(1 if (b4 == 3 and hh == 1) else 0))
                smm_g[g] = smm
                d_nm = emit(dv, lambda h, sb=sb, nmx=nmx: h.tensor_reduce(
                    fv(nmx, 4), sb[:, :].rearrange("p (g j) -> p g j", g=4),
                    AX.X, ALU.max, negate=True), waits=[(pe, smm)])
                a_e = 0
                for b4 in range(4):
                    w8 = []
                    if b4 == 0:
                        w8 = [(dv, d_nm)]
                        if ptr_g.get(g - 8):
                            w8.append((pe, ptr_g[g - 8]))
                        if ts_g.get(g - 8):
                            w8.append((dv, ts_g[g - 8]))
                    a_e = emit(ac, lambda h, b4=b4, par8=par8, nmx=nmx, sm=sm,
                               sb=sb: h.activation(
                        bfv(pn_c + 512 * par8 + 128 * b4, 128),
                        sb[:, 128 * b4:128 * b4 + 128],
                        AF.Exp, bias=fv(nmx + b4, 1), scale=1.0,
                        accum_out=fv(sm + b4, 1)), waits=w8)
                exp_g[g] = a_e
                exps_bank[par] = a_e

            def stage_b(g):
                c, bg = groups[g]
                par = g % 4
                par8 = g % 8
                sm = rowtmp_c + 4 * par8
                rs = statc_c + 32 + 4 * par
                d_rs = emit(dv, lambda h, sm=sm, rs=rs: h.reciprocal(
                    fv(rs, 4), fv(sm, 4)), waits=[(ac, exp_g[g])])
                d_ts = 0
                for b4 in range(4):
                    d_ts = emit(dv, lambda h, b4=b4, par8=par8, rs=rs:
                                h.tensor_scalar_mul(
                        bfv(pn_c + 512 * par8 + 128 * b4, 128),
                        bfv(pn_c + 512 * par8 + 128 * b4, 128),
                        fv(rs + b4, 1)),
                        waits=([(dv, d_rs)] if b4 == 0 else []))
                ts_g[g] = d_ts
                ptr = 0
                for b4 in range(4):
                    w9 = [(dv, d_ts)] if b4 == 0 else []
                    if b4 == 0 and pT_g.get(g - 2):
                        w9.append((dv, pT_g[g - 2]))
                    ptr = emit(pe, lambda h, par=par, par8=par8, b4=b4: h.transpose(
                        pbbf[:, (par % 2) * 512 + 128 * b4:
                             (par % 2) * 512 + 128 * b4 + 128],
                        bfv(pn_c + 512 * par8 + 128 * b4, 128), bfv(idbf_c, 128)),
                        waits=w9, inc=(1 if b4 == 3 else 0))
                ptr_g[g] = ptr
                wpT = [(pe, ptr)]
                if omm_g.get(g - 4):
                    wpT.append((pe, omm_g[g - 4]))
                d_pT = emit(dv, lambda h, par=par: h.tensor_copy(
                    bfv(pnT_c + 512 * par, 512),
                    pbbf[:, (par % 2) * 512:(par % 2) * 512 + 512]),
                    waits=wpT)
                pT_g[g] = d_pT
                if c1_sub == 41:
                    omm_g[g] = ptr
                    if bg == 1 and c >= 1:
                        ob_evac[c] = ob_evac[c - 1]
                    return
                omm = 0
                obank = pb[2]
                for b4 in range(4):
                    L = bg * 4 + b4
                    vcol = val_c + ((L - 1) // 2) * 1024 if L % 2 == 1 \
                        else vsh_c + (L // 2) * 1024
                    for hh in range(2):
                        w10 = [(dv, d_pT)] if (b4 == 0 and hh == 0) else []
                        if b4 == 0 and hh == 0 and bg == 0:
                            if c >= 1:
                                w10.append((ac, ob_evac[c - 1]))
                            elif t >= 1:
                                w10.append((dv, prev_proj_evac[1]))
                        omm = emit(pe, lambda h, vcol=vcol, c=c, obank=obank,
                                   L=L, hh=hh, par=par: h.matmul(
                            obank[hh * 64:hh * 64 + 64, L * 64:L * 64 + 64],
                            bfv(vcol + c * 128 + hh * 64, 64),
                            bfv(pnT_c + 512 * par + 128 * (L % 4) + hh * 64, 64),
                            start=True, stop=True), waits=w10,
                            inc=(1 if (b4 == 3 and hh == 1) else 0))
                omm_g[g] = omm
                if bg == 1:
                    ob_evac[c] = emit(ac, lambda h, c=c, obank=obank: h.activation(
                        bfv(obs_c + 512 * c, 512), obank[:, 0:512], AF.Identity),
                        waits=[(pe, omm)])

            gi = [0, 0]

            def pump_groups(max_a):
                while gi[0] < max_a:
                    n2 = min(2, max_a - gi[0])
                    for _ in range(n2):
                        stage_a(gi[0])
                        gi[0] += 1
                    if c1_sub == 40:
                        gi[1] = gi[0]
                        continue
                    while gi[1] <= gi[0] - 4 and gi[1] < 16:
                        stage_b(gi[1])
                        gi[1] += 1

            if c1_sub < 4:
                for c in range(NCH):
                    emit_q(c)
                    emit_k(c)
                c1_qkv_pe = pe.count
                c1_attn_pe[t] = pe.count
                prev_t1 = c1_ln_done[t]
                continue
            emit_q(0)
            emit_k(0)
            emit_q(1)
            emit_k(1)
            for c in range(2, NCH):
                emit_q(c)
                pump_groups(2 * (c - 1) - 1)
                emit_k(c)
                if c == NCH - 1:
                    c1_qkv_pe = pe.count
                pump_groups(2 * (c - 1))
            pump_groups(16)
            while gi[1] < 16:
                if c1_sub == 40:
                    gi[1] = 16
                    break
                stage_b(gi[1])
                gi[1] += 1
            c1_attn_pe[t] = pe.count
            if c1_sub < 5 or c1_sub in (40, 41, 42):
                prev_t1 = c1_ln_done[t]
                prev_exps_bank = dict(exps_bank)
                prev_ob_evac_last = ob_evac[NCH - 1]
                continue
            proj_evac = [0, 0]
            for co in range(NCH):
                bank_i = co % 2
                bank = pb[1 + bank_i]
                for kc in range(NCH):
                    w11 = []
                    if kc == 0:
                        w11 = [(ac, ob_evac[NCH - 1])] if co == 0 else []
                        if t == 0 and co == 0:
                            w11.append((gpms, wout_ms))
                        if proj_evac[bank_i]:
                            w11.append((dv, proj_evac[bank_i]))
                    pmm = emit(pe, lambda h, bank=bank, co=co, kc=kc: h.matmul(
                        bank[:, 0:512], bfv(WA + 24576 + kc * 1024 + co * 128, 128),
                        bfv(obs_c + 512 * kc, 512),
                        start=(kc == 0), stop=(kc == NCH - 1)), waits=w11,
                        inc=(1 if kc == NCH - 1 else 0))
                sbuf_i = co % 2
                w12 = [(pe, pmm), (syms, ld32ms)]
                if t >= 1 or co >= 2:
                    w12.append((gpms, c1_xh_st[sbuf_i]))
                d_pj = emit(dv, lambda h, bank=bank, co=co, sbuf_i=sbuf_i:
                            h.scalar_tensor_tensor(
                    fv(xhst_c[sbuf_i], 512), bank[:, 0:512],
                    fv(pp_c + PPC["b_out"] + co, 1), fv(xn32t_c + 512 * co, 512),
                    ALU.add, ALU.add), waits=w12)
                proj_evac[bank_i] = d_pj
                dma(gp, xh_s.ap()[co, :, CT * t:CT * (t + 1)],
                    fv(xhst_c[sbuf_i], 512), waits=[(dv, d_pj)])
                c1_xh_st[sbuf_i] = flush(gp)
            prev_proj_evac = [proj_evac[0], proj_evac[1]]
            prev_exps_bank = dict(exps_bank)
            prev_ob_evac_last = ob_evac[NCH - 1]
            prev_t1 = c1_ln_done[t]

        barrier()
        _stop("C1")

        w1_ms = []
        for jb in range(8):
            dma(gp, bfv(WA + jb * 4096, 4096).rearrange("p (c n) -> p c n", c=8),
                w1_d.ap()[:, jb * 512:(jb + 1) * 512].rearrange(
                    "(c p) n -> p c n", p=128))
            w1_ms.append(flush(gp))
        for g8 in range(8):
            dma(gp, bfv(WB + 4 * g8 * 1024, 4096).rearrange("p (b n) -> p b n", b=4),
                w2_d.ap()[512 * g8:512 * (g8 + 1), :].rearrange(
                    "(b p) n -> p b n", p=128))
        w2_ms = flush(gp)
        xhb_c = PHB
        sqb_c = PHB + 2048
        h2_c = PHB + 4096
        hid2_c = PHB + 6144
        xht_c = PHF
        xtm2_c = PHF + 2048
        ssc_c = PHF + 4096
        gt32_c = PHF + 4352
        sg32_c = PHF + 5376
        assert sg32_c + 1024 <= PHF + 6656

        def gtb(col, n):
            return fv(gt32_c + col // 2, n // 2).bitcast(BF16)

        def sgb(col, n):
            return fv(sg32_c + col // 2, n // 2).bitcast(BF16)

        rw_c = [rowtmp_c, r1row_c]
        r2r_c = [r2row_c, r1row_c + 512]
        rh_c = [rhs2_c, r1row_c + 768]
        r2bf_p = [r2bf_c, r2bf2_c]
        rhs2bf_p = [rhs2bf_c, rhs2bf2_c]

        c2_ldh = [0] * NB
        c2_ldg = [0] * NB
        c2_ldx = [0] * NB
        c2_sg_done = [0] * NB
        c2_store = [0] * NB
        c2_stats_dv = [0] * NB
        c2_sq_smm = [0] * NB
        c2_asq = [0] * NB
        c2_h2_done = [0] * NB
        c2_w1_last = [0] * NB
        c2_tail_pe = [0] * NB
        c2_bmm = [0] * NB
        prev_ab_dv = [0, 0]
        prev_hid2_ac = [0, 0]
        prev_w2_dv = [0, 0]

        def c2_loads(t):
            for c in range(NCH):
                dma(sy, fv(xht_c + 256 * c, 256),
                    xh_s.ap()[c, :, BT * t:BT * (t + 1)],
                    waits=([(dv, c2_sg_done[t - 1])] if c == 0 and t >= 1 else []))
            c2_ldh[t] = flush(sy)
            for c in range(NCH):
                dma(sy, gtb(256 * c, 256), g_s.ap()[t, :, c, :],
                    waits=([(dv, c2_sg_done[t - 1])] if c == 0 and t >= 1 else []))
            c2_ldg[t] = flush(sy)
            for s2 in range(2):
                dma(sy, fv(xtm2_c + 1024 * s2, 1024),
                    x_d.ap()[BT * t + 128 * s2:BT * t + 128 * (s2 + 1), :],
                    waits=([(gpms, c2_store[t - 1])] if s2 == 0 and t >= 1 else []))
            c2_ldx[t] = flush(sy)

        def c2_stats(t):
            p = t % 2
            w = [(syms, c2_ldh[t])]
            if t >= 1:
                w += [(pe, c2_sq_smm[t - 1]), (ac, c2_asq[t - 1])]
            d_bf = emit(dv, lambda h: h.tensor_copy(bfv(xhb_c, 2048),
                                                    fv(xht_c, 2048)), waits=w)
            a_sq = emit(ac, lambda h: h.activation(bfv(sqb_c, 2048),
                                                   bfv(xhb_c, 2048), AF.Square),
                        waits=[(dv, d_bf)])
            c2_asq[t] = a_sq
            smm = 0
            for c in range(NCH):
                smm = emit(pe, lambda h, c=c: h.matmul(
                    pb[0][0:1, 0:256], bfv(onescol_bf_c, 1),
                    bfv(xhb_c + 256 * c, 256),
                    start=(c == 0), stop=(c == NCH - 1)),
                    waits=([(dv, d_bf)] +
                           ([(dv, c2_stats_dv[t - 1])] if t >= 1 else []))
                    if c == 0 else [], inc=(1 if c == NCH - 1 else 0))
            for c in range(NCH):
                smm = emit(pe, lambda h, c=c: h.matmul(
                    pb[0][0:1, 256:512], bfv(onescol_bf_c, 1),
                    bfv(sqb_c + 256 * c, 256),
                    start=(c == 0), stop=(c == NCH - 1)),
                    waits=([(ac, a_sq)] if c == 0 else []),
                    inc=(1 if c == NCH - 1 else 0))
            c2_sq_smm[t] = smm
            rw = rw_c[p]
            r2r = r2r_c[p]
            rh = rh_c[p]
            wrm = [(pe, smm)]
            if t >= 2:
                wrm.append((pe, c2_bmm[t - 2]))
            d_p2 = emit(dv, lambda h, rw=rw: h.tensor_scalar_mul(
                fv(rw, 256, 0, 1), pb[0][0:1, 0:256], 1.0 / C), waits=wrm)
            d_p2 = emit(dv, lambda h, rw=rw: h.tensor_scalar_mul(
                fv(rw + 256, 256, 0, 1), pb[0][0:1, 256:512], 1.0 / C),
                waits=[(dv, d_p2)])
            d_p2 = emit(dv, lambda h, rw=rw, r2r=r2r: h.tensor_mul(
                fv(r2r, 256, 0, 1), fv(rw, 256, 0, 1), fv(rw, 256, 0, 1)),
                waits=[(dv, d_p2)])
            d_v2 = emit(dv, lambda h, rw=rw, r2r=r2r: h.tensor_sub(
                fv(rw + 256, 256, 0, 1), fv(rw + 256, 256, 0, 1),
                fv(r2r, 256, 0, 1)), waits=[(dv, d_p2)])
            c2_stats_dv[t] = d_v2
            a_sd = emit(ac, lambda h, rw=rw: h.activation(
                fv(rw + 256, 256, 0, 1), fv(rw + 256, 256, 0, 1), AF.Sqrt,
                bias=fv(statc_c + 63, 1, 0, 1), scale=1.0), waits=[(dv, d_v2)])
            d_r2 = emit(dv, lambda h, rw=rw, r2r=r2r: h.reciprocal(
                fv(r2r, 256, 0, 1), fv(rw + 256, 256, 0, 1)), waits=[(ac, a_sd)])
            d_ms = emit(dv, lambda h, rh=rh: h.memset(fv(rh, 256, 0, 2), -1.0),
                        waits=[(dv, d_r2)])
            d_rh = emit(dv, lambda h, rw=rw, r2r=r2r, rh=rh: h.tensor_mul(
                fv(rh, 256, 0, 1), fv(rw, 256, 0, 1), fv(r2r, 256, 0, 1)),
                waits=[(dv, d_ms)])
            a_r2b = emit(ac, lambda h, p=p, r2r=r2r: h.copy(
                bfv(r2bf_p[p], 256, 0, 1), fv(r2r, 256, 0, 1)),
                waits=[(dv, d_r2)])
            a_rhb = emit(ac, lambda h, p=p, rh=rh: h.copy(
                bfv(rhs2bf_p[p], 256, 0, 2), fv(rh, 256, 0, 2)),
                waits=[(dv, d_rh)])
            d_h2 = 0
            bmm = 0
            for c in range(NCH):
                bank_i = c % 2
                bank = pb[1 + bank_i]
                w14 = [(ac, a_r2b), (ac, a_rhb)] if c == 0 else []
                if c == 0 and t >= 1:
                    w14.append((pe, c2_w1_last[t - 1]))
                if prev_ab_dv[bank_i]:
                    w14.append((dv, prev_ab_dv[bank_i]))
                bmm = emit(pe, lambda h, bank=bank, c=c, p=p: h.matmul(
                    bank[:, 0:256], bfv(gb2bf_c + 128 * c, 128, 0, 1),
                    bfv(r2bf_p[p], 256, 0, 1), start=True, stop=True), waits=w14)
                bmm = emit(pe, lambda h, bank=bank, c=c, p=p: h.matmul(
                    bank[:, 256:512], bfv(gb2bf_c + 128 * c, 128, 0, 2),
                    bfv(rhs2bf_p[p], 256, 0, 2), start=True, stop=True))
                d_tmp = emit(dv, lambda h, bank=bank, c=c: h.tensor_mul(
                    fv(ssc_c, 256), fv(xht_c + 256 * c, 256), bank[:, 0:256]),
                    waits=[(pe, bmm)])
                d_h2 = emit(dv, lambda h, bank=bank, c=c: h.tensor_sub(
                    bfv(h2_c + 256 * c, 256), fv(ssc_c, 256), bank[:, 256:512]),
                    waits=[(dv, d_tmp)])
                prev_ab_dv[bank_i] = d_h2
            c2_bmm[t] = bmm
            c2_h2_done[t] = d_h2

        c2_loads(0)
        c2_stats(0)
        for t in range(NB):
            mm1 = 0
            for pr in range(16):
                bank_i = pr % 2
                bank = pb[3 + bank_i]
                for half in range(2):
                    ngc = pr * 2 + half
                    for kc in range(NCH):
                        w15 = []
                        if kc == 0 and half == 0:
                            w15 = [(dv, c2_h2_done[t])] if pr == 0 else []
                            if t == 0:
                                w15.append((gpms, w1_ms[pr // 2]))
                            if prev_hid2_ac[bank_i]:
                                w15.append((ac, prev_hid2_ac[bank_i]))
                        mm1 = emit(pe, lambda h, bank=bank, half=half, ngc=ngc,
                                   kc=kc: h.matmul(
                            bank[:, half * 256:half * 256 + 256],
                            bfv(WA + (ngc // 4) * 4096 + kc * 512 + (ngc % 4) * 128,
                                128),
                            bfv(h2_c + 256 * kc, 256),
                            start=(kc == 0), stop=(kc == NCH - 1)), waits=w15,
                            inc=(1 if (kc == NCH - 1 and half == 1) else 0))
                ev = 0
                for half in range(2):
                    ngc = pr * 2 + half
                    ev = emit(ac, lambda h, bank=bank, half=half, ngc=ngc:
                              h.activation(
                        bfv(hid2_c + 256 * ngc, 256),
                        bank[:, half * 256:half * 256 + 256], AF.Gelu,
                        bias=fv(pp_c + PPC["b1"] + ngc, 1), scale=1.0),
                        waits=[(pe, mm1)] if half == 0 else [])
                prev_hid2_ac[bank_i] = ev
            c2_w1_last[t] = pe.count
            d_sg = 0
            for c in range(NCH):
                bank_i = c % 2
                bank = pb[5 + bank_i]
                for kc in range(F // 128):
                    w16 = []
                    if kc == 0:
                        w16 = [(ac, prev_hid2_ac[0]), (ac, prev_hid2_ac[1])] \
                            if c == 0 else []
                        if t == 0 and c == 0:
                            w16.append((gpms, w2_ms))
                        if prev_w2_dv[bank_i]:
                            w16.append((dv, prev_w2_dv[bank_i]))
                    mm2 = emit(pe, lambda h, bank=bank, c=c, kc=kc: h.matmul(
                        bank[:, 0:256], bfv(WB + kc * 1024 + c * 128, 128),
                        bfv(hid2_c + 256 * kc, 256),
                        start=(kc == 0), stop=(kc == F // 128 - 1)), waits=w16,
                        inc=(1 if kc == F // 128 - 1 else 0))
                d_s = emit(dv, lambda h, bank=bank, c=c: h.scalar_tensor_tensor(
                    fv(ssc_c, 256), bank[:, 0:256], fv(pp_c + PPC["b2"] + c, 1),
                    fv(xht_c + 256 * c, 256), ALU.add, ALU.add),
                    waits=[(pe, mm2)])
                prev_w2_dv[bank_i] = d_s
                wsg = [(dv, d_s)]
                if c == 0:
                    wsg.append((syms, c2_ldg[t]))
                    if t >= 1:
                        wsg.append((pe, c2_tail_pe[t - 1]))
                d_sg = emit(dv, lambda h, c=c: h.tensor_mul(
                    sgb(256 * c, 256), fv(ssc_c, 256), gtb(256 * c, 256)),
                    waits=wsg)
            c2_sg_done[t] = d_sg
            if t + 1 < NB:
                c2_loads(t + 1)
            dvo = 0
            for s2 in range(2):
                ptr = 0
                for c in range(NCH):
                    w17 = []
                    if c == 0:
                        w17 = [(dv, d_sg)]
                        if t >= 1 or s2 == 1:
                            w17.append((dv, dvo))
                    ptr = emit(pe, lambda h, c=c, s2=s2: h.transpose(
                        pbbf[:, c * 128:c * 128 + 128],
                        sgb(256 * c + 128 * s2, 128), bfv(idbf_c, 128)),
                        waits=w17, inc=(1 if c == NCH - 1 else 0))
                for c in range(NCH):
                    w18 = [(pe, ptr), (syms, c2_ldx[t])] if c == 0 else []
                    dvo = emit(dv, lambda h, c=c, s2=s2: h.tensor_add(
                        fv(xtm2_c + 1024 * s2 + 128 * c, 128),
                        fv(xtm2_c + 1024 * s2 + 128 * c, 128),
                        pbbf[:, c * 128:c * 128 + 128]), waits=w18)
                dma(gp, y_d.ap()[BT * t + 128 * s2:BT * t + 128 * (s2 + 1), :],
                    fv(xtm2_c + 1024 * s2, 1024), waits=[(dv, dvo)])
            c2_tail_pe[t] = pe.count
            c2_store[t] = flush(gp)
            if t + 1 < NB:
                c2_stats(t + 1)

        flush(sy)
        flush(gp)

        with nc.Block() as block:
            @block.sync
            def _(h):
                for fn in sy.stream:
                    fn(h)

            @block.gpsimd
            def _(h):
                for fn in gp.stream:
                    fn(h)

            @block.tensor
            def _(h):
                for fn in pe.stream:
                    fn(h)

            @block.vector
            def _(h):
                for fn in dv.stream:
                    fn(h)

            @block.scalar
            def _(h):
                for fn in ac.stream:
                    fn(h)

    return nc


def _pack_params(ins):
    cols = []
    for nm, key in [("ln1_g", "L0_attn_ln1_g"), ("ln1_b", "L0_attn_ln1_b"),
                    ("ln2_g", "L0_attn_ln2_g"), ("ln2_b", "L0_attn_ln2_b"),
                    ("lnf_g", "L0_ffn_ln_gamma"), ("lnf_b", "L0_ffn_ln_beta"),
                    ("b_qkv", "L0_attn_b_qkv"), ("b_out", "L0_attn_b_out"),
                    ("b1", "L0_attn_b1"), ("b2", "L0_attn_b2"),
                    ("fc1b", "L0_ffn_fc1_bias"), ("fc2b", "L0_ffn_fc2_bias")]:
        v = np.asarray(ins[key], dtype=np.float32)
        cols.append(v.reshape(-1, 128).T)
    return np.ascontiguousarray(np.concatenate(cols, axis=1))


def _mask128():
    i = np.arange(64)
    j = np.arange(128)
    allowed = (j[None, :] < 64) | ((j[None, :] - 64) <= i[:, None])
    m64 = np.where(allowed, 0.0, NEG).astype(np.float32)
    return np.ascontiguousarray(np.concatenate([m64, m64], axis=0))


def kernel(**ins):
    x = np.ascontiguousarray(np.asarray(ins["L0_attn_x"], dtype=np.float32))
    shared = {
        "wqkv": np.ascontiguousarray(np.asarray(ins["L0_attn_w_qkv"], np.float32)),
        "wout": np.ascontiguousarray(np.asarray(ins["L0_attn_w_out"], np.float32)),
        "w1": np.ascontiguousarray(np.asarray(ins["L0_attn_w1"], np.float32)),
        "w2": np.ascontiguousarray(np.asarray(ins["L0_attn_w2"], np.float32)),
        "fc1q": np.ascontiguousarray(np.asarray(ins["L0_ffn_fc1_q"], np.int32)),
        "fc2q": np.ascontiguousarray(np.asarray(ins["L0_ffn_fc2_q"], np.int32)),
        "fc1s": np.ascontiguousarray(np.asarray(ins["L0_ffn_fc1_scales"], np.float32)),
        "fc2s": np.ascontiguousarray(np.asarray(ins["L0_ffn_fc2_scales"], np.float32)),
        "pp": _pack_params(ins),
        "gb2": np.ascontiguousarray(np.stack([
            np.asarray(ins["L0_attn_ln2_g"], np.float32),
            np.asarray(ins["L0_attn_ln2_b"], np.float32)])),
        "bv": np.ascontiguousarray(
            np.asarray(ins["L0_attn_b_qkv"], np.float32)[2048:3072][None, :]),
        "ident": np.eye(128, dtype=np.float32),
        "mask": _mask128(),
    }
    nc = build_nc()
    in_maps = []
    for core in range(8):
        m = dict(shared)
        m["x"] = x[core]
        in_maps.append(m)
    res = run_bass_kernel_spmd(nc, in_maps, core_ids=list(range(8)))
    out = np.stack([np.asarray(r["y"], np.float32) for r in res.results], axis=0)
    return out


if __name__ == "__main__":
    import reference as _r
    ins = _r.setup_inputs()
    out = kernel(**{k: np.asarray(v) for k, v in ins.items()})
    ref = np.asarray(_r.reference(**ins))
    err = np.abs(out - ref)
    denom = np.square(ref).mean()
    print("resvar:", float(np.square(out - ref).mean() / denom),
          "maxabs:", float(err.max()))


# version 18 (final)
# speedup vs baseline: 1.0553x; 1.0050x over previous
import numpy as np
import ml_dtypes
from contextlib import ExitStack

import concourse.bass as bass
import concourse.mybir as mybir
from concourse.bass_utils import run_bass_kernel_spmd

F32 = mybir.dt.float32
BF16 = mybir.dt.bfloat16
I32 = mybir.dt.int32
AF = mybir.ActivationFunctionType
ALU = mybir.AluOpType
AX = mybir.AxisListType

T = 2048
C = 1024
NCH = 8
F = 4096
EPS = 1e-5
NA = 16
BT = 256
NB = T // BT
CT = 512
NC1 = T // CT
NEG = -1e9


class Eng:
    def __init__(self, name):
        self.name = name
        self.count = 0
        self.stream = []
        self.semh = None
        self.ms = None


def build_nc(upto="C2", dbg_scratch=False, c1_tiles=NC1, c1_sub=9):
    nc = bass.Bass("TRN2", target_bir_lowering=False)

    x_d = nc.dram_tensor("x", [T, C], F32, kind="ExternalInput")
    wqkv_d = nc.dram_tensor("wqkv", [C, 3 * C], F32, kind="ExternalInput")
    wout_d = nc.dram_tensor("wout", [C, C], F32, kind="ExternalInput")
    w1_d = nc.dram_tensor("w1", [C, F], F32, kind="ExternalInput")
    w2_d = nc.dram_tensor("w2", [F, C], F32, kind="ExternalInput")
    fc1q_d = nc.dram_tensor("fc1q", [C, F], I32, kind="ExternalInput")
    fc2q_d = nc.dram_tensor("fc2q", [F, C], I32, kind="ExternalInput")
    fc1s_d = nc.dram_tensor("fc1s", [C // 128, F], F32, kind="ExternalInput")
    fc2s_d = nc.dram_tensor("fc2s", [F // 128, C], F32, kind="ExternalInput")
    pp_d = nc.dram_tensor("pp", [128, 160], F32, kind="ExternalInput")
    PPC = {}
    off = 0
    for nm, w in [("ln1_g", 8), ("ln1_b", 8), ("ln2_g", 8), ("ln2_b", 8),
                  ("lnf_g", 8), ("lnf_b", 8), ("b_qkv", 24), ("b_out", 8),
                  ("b1", 32), ("b2", 8), ("fc1b", 32), ("fc2b", 8)]:
        PPC[nm] = off
        off += w
    gb2_d = nc.dram_tensor("gb2", [2, C], F32, kind="ExternalInput")
    bv_d = nc.dram_tensor("bv", [1, C], F32, kind="ExternalInput")
    id_d = nc.dram_tensor("ident", [128, 128], F32, kind="ExternalInput")
    mask_d = nc.dram_tensor("mask", [128, 128], F32, kind="ExternalInput")
    y_d = nc.dram_tensor("y", [T, C], F32, kind="ExternalOutput")

    _sk = "ExternalOutput" if dbg_scratch else "Internal"
    xn32_s = nc.dram_tensor("xn32_s", [NA, 128, NCH, 128], F32, kind=_sk)
    xnbf_s = nc.dram_tensor("xnbf_s", [NA, 128, NCH, 128], BF16, kind=_sk)
    g_s = nc.dram_tensor("g_s", [NB, 128, NCH, BT], BF16, kind=_sk)
    xh_s = nc.dram_tensor("xh_s", [NCH, 128, T], F32, kind=_sk)
    dbgst_d = nc.dram_tensor("dbgst", [128, 96], F32, kind=_sk)
    r1_s = nc.dram_tensor("r1_s", [1, 2048], F32, kind="Internal")

    with ExitStack() as stk:
        WA = 0
        WB = 32768
        PHB = 65536
        PERB = PHB + 14336
        idbf_c = PERB
        onesrow_bf_c = PERB + 128
        onescol_bf_c = PERB + 256
        bvrow_bf_c = PERB + 260
        gb2bf_c = PERB + 1284
        r2bf_c = PERB + 2308
        rhs2bf_c = PERB + 2564
        maskbf_c = PERB + 2880
        r2bf2_c = PERB + 3040
        rhs2bf2_c = PERB + 3296
        NBF = PERB + 3552
        abf = stk.enter_context(nc.sbuf_tensor("abf", [128, NBF], BF16))

        id32_c = 0
        mask_c = 128
        pp_c = 640
        halfb_c = 800
        bqs_c = 808
        onesrow32_c = 816
        r1row_c = 944
        r2row_c = 2992
        rhs2_c = 3248
        rowtmp_c = 3504
        statc_c = 4016
        PHF = 4080
        NF32 = PHF + 6656
        af32 = stk.enter_context(nc.sbuf_tensor("af32", [128, NF32], F32))
        assert NBF * 2 + NF32 * 4 <= 211000, (NBF * 2, NF32 * 4)

        pb = [stk.enter_context(nc.psum_tensor(f"pb{i}", [128, 512], F32))
              for i in range(7)]
        pbbf = stk.enter_context(nc.psum_tensor("pbbf", [128, 1024], BF16))

        sy = Eng("sy"); gp = Eng("gp"); pe = Eng("pe"); dv = Eng("dv"); ac = Eng("ac")
        ALLE = (sy, gp, pe, dv, ac)
        for e in ALLE:
            e.semh = stk.enter_context(nc.semaphore("sem_" + e.name))
        syms = Eng("syms"); gpms = Eng("gpms")
        for e in (syms, gpms):
            e.semh = stk.enter_context(nc.semaphore("sem_" + e.name))
        sy.ms, gp.ms = syms, gpms

        DIS = [False]
        PH_ORDER = ["P0", "AL1", "AL2", "AL3", "AL4", "A1", "A2", "A", "DQ", "B", "C1", "C2"]

        def _stop(name):
            if PH_ORDER.index(upto) <= PH_ORDER.index(name):
                DIS[0] = True

        def emit(e, body, waits=(), inc=1):
            if DIS[0]:
                return e.count
            for (we, wv) in waits:
                assert wv <= we.count, ("wait ahead of emission",
                                        e.name, we.name, wv, we.count)
            if inc:
                e.count += inc
            wl = [(we.semh, wv) for (we, wv) in waits if wv > 0]
            semh = e.semh

            def fn(h, wl=wl, body=body, inc=inc, semh=semh):
                for s, v in wl:
                    h.wait_ge(s, v)
                last = body(h)
                if inc:
                    assert last is not None, "inc step body must return instruction"
                    last.then_inc(semh, inc)
            e.stream.append(fn)
            return e.count

        def dma(e, out_ap, in_ap, waits=()):
            return emit(e, (lambda h, o=out_ap, i=in_ap:
                            h.dma_start(out=o, in_=i)), waits=waits, inc=16)

        def flush(q):
            if DIS[0]:
                return q.ms.count
            q.ms.count += 1
            rv = q.count
            def fn(h, raw=q.semh, rv=rv, msh=q.ms.semh):
                h.wait_ge(raw, rv)
                h.sem_inc(msh, 1)
            q.stream.append(fn)
            return q.ms.count

        def barrier():
            flush(sy)
            flush(gp)
            counts = [(e, e.count) for e in (pe, dv, ac)] + \
                     [(syms, syms.count), (gpms, gpms.count)]
            for e in ALLE:
                ws = [(o, c) for (o, c) in counts if o is not e and c > 0]
                emit(e, lambda h: None, waits=ws, inc=0)

        def bfv(col, n, p0=0, p1=128):
            return abf[p0:p1, col:col + n]

        def fv(col, n, p0=0, p1=128):
            return af32[p0:p1, col:col + n]

        ld_pp = dma(sy, fv(pp_c, 160), pp_d.ap())
        dma(sy, fv(id32_c, 128), id_d.ap())
        ld_bv = dma(sy, fv(PHF, 1024, 0, 1), bv_d.ap())
        ld_gb2 = dma(sy, fv(PHF + 1024, 1024, 0, 2), gb2_d.ap())
        p0_syms = flush(sy)
        dma(gp, bfv(idbf_c, 128), id_d.ap())
        p0_gpms = flush(gp)

        emit(dv, lambda h: h.memset(fv(statc_c, 48), 0.0))
        emit(dv, lambda h: h.memset(fv(statc_c + 63, 1), EPS))
        emit(dv, lambda h: h.memset(fv(onesrow32_c, 128, 0, 1), 1.0))
        emit(dv, lambda h: h.memset(bfv(onesrow_bf_c, 128, 0, 1), 1.0))
        emit(dv, lambda h: h.memset(bfv(onescol_bf_c, 1), 1.0))
        emit(dv, lambda h: h.tensor_scalar_mul(fv(halfb_c, 8),
                                               fv(pp_c + PPC["fc2b"], 8), 0.5),
             waits=[(syms, p0_syms)])
        p0_dv = emit(dv, lambda h: h.tensor_scalar_mul(fv(bqs_c, 8),
                                                       fv(pp_c + PPC["b_qkv"], 8), 0.125))
        p0_bvrow = emit(ac, lambda h: h.copy(bfv(bvrow_bf_c, 1024, 0, 1),
                                             fv(PHF, 1024, 0, 1)),
                        waits=[(syms, p0_syms)])
        p0_ones = p0_dv
        p0_bvmm = emit(pe, lambda h: h.matmul(
            pb[3][:, 0:512], bfv(onesrow_bf_c, 128, 0, 1),
            bfv(bvrow_bf_c, 512, 0, 1), start=True, stop=True),
            waits=[(ac, p0_bvrow), (dv, p0_dv)])
        p0_bvmm = emit(pe, lambda h: h.matmul(
            pb[4][:, 0:512], bfv(onesrow_bf_c, 128, 0, 1),
            bfv(bvrow_bf_c + 512, 512, 0, 1), start=True, stop=True))
        emit(ac, lambda h: h.copy(bfv(bvrow_bf_c, 512), pb[3][:, 0:512]),
             waits=[(pe, p0_bvmm)])
        p0_bvb = emit(ac, lambda h: h.copy(bfv(bvrow_bf_c + 512, 512), pb[4][:, 0:512]))
        p0_ac = emit(ac, lambda h: h.copy(bfv(gb2bf_c, 1024, 0, 2),
                                          fv(PHF + 1024, 1024, 0, 2)),
                     waits=[(syms, p0_syms)])

        _stop("P0")
        xtm_c = [PHF, PHF + 1024]
        sq_c = [PHF + 2048, PHF + 3072]
        xntm_c = PHF + 2048
        stg32_c = [PHF + 3072, PHF + 4096]
        s1p, ssqp, mup, varp, r0p, mbp, u0p, r1p = \
            [PHF + 5120 + 16 * k for k in range(8)]
        stgbf_c = [PHB, PHB + 1024]
        qdq_c = [PHB + 2048, PHB + 6144]
        srow_c = PHB + 10240

        dq_jobs = [("fc1", k) for k in range(8)] + [("fc2", g) for g in range(8)]
        dq_done = [0, 0]
        bank_done = [0, 0]
        srow_pe = 0
        a_ssq_done = [0] * NA
        asq_done = [0] * NA
        for i in range(NA):
            bi = i % 2
            w = []
            if i >= 2:
                w = [(dv, a_ssq_done[i - 2]), (ac, asq_done[i - 2])]
            ld = dma(sy, fv(xtm_c[bi], 1024), x_d.ap()[128 * i:128 * (i + 1), :],
                     waits=w)
            ldms = flush(sy)
            xt = fv(xtm_c[bi], 1024)
            d_s1 = emit(dv, lambda h, xt=xt, i=i: h.reduce_sum(
                fv(s1p + i, 1), xt, AX.X), waits=[(syms, ldms)])
            a_sq = emit(ac, lambda h, xt=xt, bi=bi: h.activation(
                fv(sq_c[bi], 1024), xt, AF.Square),
                waits=[(syms, ldms)] +
                      ([(dv, a_ssq_done[i - 2])] if i >= 2 else []))
            asq_done[i] = a_sq
            a_ssq_done[i] = emit(dv, lambda h, i=i, bi=bi: h.reduce_sum(
                fv(ssqp + i, 1), fv(sq_c[bi], 1024), AX.X),
                waits=[(ac, a_sq)])
            mat, kc = dq_jobs[i]
            q_eng = gp
            if mat == "fc1":
                qsrc = fc1q_d.ap()[128 * kc:128 * (kc + 1), :]
                ssrc = fc1s_d.ap()[kc:kc + 1, :]
                qdst = bfv(qdq_c[bi], 4096)
                sdst = bfv(srow_c, 4096, 0, 1)
                dsts = [WA + kc * 4096 + q5 * 512 for q5 in range(8)]
            else:
                qsrc = fc2q_d.ap()[512 * kc:512 * (kc + 1), :].rearrange(
                    "(b p) n -> p b n", p=128)
                ssrc = fc2s_d.ap().rearrange("(a b) n -> a b n", b=4)[kc:kc + 1]
                qdst = bfv(qdq_c[bi], 4096).rearrange("p (b n) -> p b n", b=4)
                sdst = bfv(srow_c, 4096, 0, 1).rearrange("a (b n) -> a b n", b=4)
                dsts = [WB + (4 * kc + q5 // 2) * 1024 + (q5 % 2) * 512
                        for q5 in range(8)]
            dma(q_eng, qdst, qsrc,
                waits=([(dv, dq_done[bi])] if dq_done[bi] else []))
            dma(q_eng, sdst, ssrc,
                waits=([(pe, srow_pe)] if srow_pe else []))
            dqms = flush(q_eng)
            dq_sem = q_eng.ms
            for q5 in range(8):
                bk = q5 % 2
                pw = [(dq_sem, dqms)]
                if i == 0 and q5 < 2:
                    pw.append((ac, p0_bvb))
                if bank_done[bk]:
                    pw.append((dv, bank_done[bk]))
                srow_pe = emit(pe, lambda h, bk=bk, q5=q5: h.matmul(
                    pb[3 + bk][:, 0:512], bfv(onesrow_bf_c, 128, 0, 1),
                    bfv(srow_c, 4096, 0, 1)[0:1, q5 * 512:(q5 + 1) * 512],
                    start=True, stop=True),
                    waits=pw)
                bank_done[bk] = emit(dv, lambda h, dst=dsts[q5], q5=q5, bi=bi,
                                     bk=bk: h.scalar_tensor_tensor(
                    bfv(dst, 512), bfv(qdq_c[bi] + q5 * 512, 512),
                    -128.0, pb[3 + bk][:, 0:512], ALU.add, ALU.mult),
                    waits=[(pe, srow_pe), (dq_sem, dqms)])
            dq_done[bi] = bank_done[1]

        d_m = emit(dv, lambda h: h.tensor_scalar_mul(fv(mup, 16), fv(s1p, 16),
                                                     1.0 / C))
        d_m = emit(dv, lambda h: h.tensor_scalar_mul(fv(varp, 16), fv(ssqp, 16),
                                                     1.0 / C), waits=[(dv, d_m)])
        d_m = emit(dv, lambda h: h.tensor_mul(fv(s1p, 16), fv(mup, 16),
                                              fv(mup, 16)), waits=[(dv, d_m)])
        d_var = emit(dv, lambda h: h.tensor_sub(fv(varp, 16), fv(varp, 16),
                                                fv(s1p, 16)), waits=[(dv, d_m)])
        a_sd = emit(ac, lambda h: h.activation(fv(s1p, 16), fv(varp, 16), AF.Sqrt,
                                               bias=fv(statc_c + 63, 1), scale=1.0),
                    waits=[(dv, d_var)])
        d_r0 = emit(dv, lambda h: h.reciprocal(fv(r0p, 16), fv(s1p, 16)),
                    waits=[(ac, a_sd)])
        d_mb = emit(dv, lambda h: h.scalar_tensor_tensor(
            fv(mbp, 16), fv(mup, 16), -1.0, fv(r0p, 16), ALU.mult, ALU.mult),
            waits=[(dv, d_r0)])
        d_u = emit(dv, lambda h: h.tensor_mul(fv(u0p, 16), fv(varp, 16),
                                              fv(r0p, 16)), waits=[(dv, d_r0)])
        d_u = emit(dv, lambda h: h.tensor_mul(fv(u0p, 16), fv(u0p, 16),
                                              fv(r0p, 16)), waits=[(dv, d_u)])
        a_u = emit(ac, lambda h: h.activation(fv(u0p, 16), fv(u0p, 16), AF.Sqrt,
                                              bias=fv(statc_c + 63, 1), scale=1.0),
                   waits=[(dv, d_u)])
        d_r1 = emit(dv, lambda h: h.reciprocal(fv(r1p, 16), fv(u0p, 16)),
                    waits=[(ac, a_u)])
        p_r1t = emit(pe, lambda h: h.transpose(pb[2][0:16, 0:128], fv(r1p, 16),
                                               fv(id32_c, 128)),
                     waits=[(dv, d_r1)])
        r1stg = PHF + 5248
        d_r1t = emit(dv, lambda h, r1stg=r1stg: h.tensor_copy(
            fv(r1stg, 128, 0, 16), pb[2][0:16, 0:128]),
            waits=[(pe, p_r1t)])
        dma(gp, r1_s.ap().rearrange("a (i p) -> (a i) p", i=16),
            fv(r1stg, 128, 0, 16), waits=[(dv, d_r1t)])
        r1out_ms = flush(gp)
        ld_r1 = dma(sy, fv(r1row_c, 2048, 0, 1), r1_s.ap(),
                    waits=[(gpms, r1out_ms)])
        r1in_ms = flush(sy)

        a_n_done = [0] * NA
        a_tr_done = [0] * NA
        a_ev_done = [0] * NA
        a_bf_done = [0] * NA
        a_st_done = [0] * NA
        for i in range(NA):
            bi = i % 2
            w = [(dv, d_r1)]
            if i >= 2:
                w.append((ac, a_n_done[i - 2]))
            else:
                w.append((ac, asq_done[NA - 2 + i]))
            ld = dma(sy, fv(xtm_c[bi], 1024), x_d.ap()[128 * i:128 * (i + 1), :],
                     waits=w)
            ldms = flush(sy)
            xt = fv(xtm_c[bi], 1024)
            a_n = emit(ac, lambda h, xt=xt, i=i: h.activation(
                fv(xntm_c, 1024), xt, AF.Identity,
                bias=fv(mbp + i, 1), scale=fv(r0p + i, 1)),
                waits=[(syms, ldms)] +
                      ([(pe, a_tr_done[i - 1])] if i >= 1 else []))
            a_n_done[i] = a_n
            tw = [(ac, a_n)]
            if i >= 1:
                tw += [(dv, a_ev_done[i - 1]), (ac, a_bf_done[i - 1])]
            p_tr = 0
            for c8 in range(8):
                bank = pb[c8 // 4]
                p_tr = emit(pe, lambda h, c8=c8, bank=bank: h.transpose(
                    bank[:, (c8 % 4) * 128:(c8 % 4) * 128 + 128],
                    fv(xntm_c + c8 * 128, 128), fv(id32_c, 128)),
                    waits=(tw if c8 == 0 else []), inc=(1 if c8 == 7 else 0))
            a_tr_done[i] = p_tr
            ew = [(pe, p_tr)]
            if i >= 2:
                ew += [(gpms, a_st_done[i - 2])]
            emit(dv, lambda h, bi=bi: h.tensor_copy(fv(stg32_c[bi], 512),
                                                    pb[0][:, :]), waits=ew)
            d_ev = emit(dv, lambda h, bi=bi: h.tensor_copy(
                fv(stg32_c[bi] + 512, 512), pb[1][:, :]))
            a_ev_done[i] = d_ev
            a_be = emit(ac, lambda h, bi=bi: h.copy(bfv(stgbf_c[bi], 1024),
                                                    fv(stg32_c[bi], 1024)),
                        waits=[(dv, d_ev)])
            a_bf_done[i] = a_be
            dma(gp, xn32_s.ap()[i].rearrange("p c n -> p (c n)"),
                fv(stg32_c[bi], 1024), waits=[(dv, d_ev), (ac, a_be)])
            dma(gp, xnbf_s.ap()[i].rearrange("p c n -> p (c n)"),
                bfv(stgbf_c[bi], 1024))
            a_st_done[i] = flush(gp)

        barrier()
        _stop("A")
        _stop("DQ")

        xnb_c = [PHB, PHB + 2048]
        hfb_c = PHB + 4096
        hid_c = PHB + 6144
        t1_c = PHF
        gst_c = PHF + 512

        b_ln_done = [0] * NB
        b_res_done = [0] * NB
        b_gst_done = [0] * NB
        b_ld = [0] * NB
        prev_r1b_pe = 0
        prev_hid_pe = [0, 0]
        fc1_evac = [0, 0]
        fc2_evac = [0, 0]
        b_hf_done = 0
        for t in range(NB):
            bi = t % 2
            w = []
            if t >= 2:
                w = [(dv, b_res_done[t - 2])]
            for c in range(NCH):
                dma(sy, bfv(xnb_c[bi] + 256 * c, 256).rearrange(
                        "p (k n) -> p k n", k=2),
                    xnbf_s.ap()[2 * t:2 * t + 2, :, c, :].rearrange(
                        "k p n -> p k n"),
                    waits=(w if c == 0 else []))
            ldxms = flush(sy)
            r1b = emit(pe, lambda h, t=t: h.matmul(
                pb[0][:, 0:256], fv(onesrow32_c, 128, 0, 1),
                fv(r1row_c + BT * t, 256, 0, 1), start=True, stop=True),
                waits=[(dv, b_ln_done[t - 1])] if t >= 1 else [(dv, p0_dv)])
            dvl = 0
            acl = 0
            for c in range(NCH):
                dvl = emit(dv, lambda h, bi=bi, c=c: h.tensor_mul(
                    fv(t1_c, 256), bfv(xnb_c[bi] + 256 * c, 256), pb[0][:, 0:256]),
                    waits=[(pe, r1b), (syms, ldxms)] if c == 0 else
                          [(ac, acl)])
                acl = emit(ac, lambda h, c=c: h.activation(
                    bfv(hfb_c + 256 * c, 256), fv(t1_c, 256), AF.Identity,
                    bias=fv(pp_c + PPC["lnf_b"] + c, 1),
                    scale=fv(pp_c + PPC["lnf_g"] + c, 1)),
                    waits=[(dv, dvl)] +
                          ([(pe, b_fc1_last)] if t >= 1 and c == 0 else []))
            b_ln_done[t] = dvl
            b_hf_done = acl
            for pr in range(16):
                bank_i = pr % 2
                bank = pb[1 + bank_i]
                for half in range(2):
                    ngc = pr * 2 + half
                    for kc in range(NCH):
                        w2_ = []
                        if kc == 0 and half == 0:
                            w2_ = [(ac, b_hf_done if pr == 0 else b_hf_done)]
                            if fc1_evac[bank_i]:
                                w2_.append((ac, fc1_evac[bank_i]))
                        b_fc1_last = emit(pe, lambda h, bank=bank, half=half,
                                          ngc=ngc, kc=kc: h.matmul(
                            bank[:, half * 256:half * 256 + 256],
                            bfv(WA + kc * 4096 + ngc * 128, 128),
                            bfv(hfb_c + 256 * kc, 256),
                            start=(kc == 0), stop=(kc == NCH - 1)),
                            waits=w2_,
                            inc=(1 if (kc == NCH - 1 and half == 1) else 0))
                ev = 0
                for half in range(2):
                    ngc = pr * 2 + half
                    ev = emit(ac, lambda h, bank=bank, half=half, ngc=ngc: h.activation(
                        bfv(hid_c + 256 * ngc, 256),
                        bank[:, half * 256:half * 256 + 256], AF.Silu,
                        bias=fv(pp_c + PPC["fc1b"] + ngc, 1), scale=1.0),
                        waits=[(pe, b_fc1_last)] if half == 0 else [])
                fc1_evac[bank_i] = ev
            dres = 0
            for c in range(NCH):
                bank_i = c % 2
                bank = pb[3 + bank_i]
                for kc in range(F // 128):
                    w3 = []
                    if kc == 0:
                        w3 = [(ac, fc1_evac[0]), (ac, fc1_evac[1])] if c == 0 else []
                        if fc2_evac[bank_i]:
                            w3.append((dv, fc2_evac[bank_i]))
                    b_fc2_last = emit(pe, lambda h, bank=bank, c=c, kc=kc: h.matmul(
                        bank[:, 0:256],
                        bfv(WB + kc * 1024 + c * 128, 128),
                        bfv(hid_c + 256 * kc, 256),
                        start=(kc == 0), stop=(kc == F // 128 - 1)),
                        waits=w3, inc=(1 if kc == F // 128 - 1 else 0))
                d_u = emit(dv, lambda h, bank=bank, c=c: h.tensor_scalar(
                    fv(t1_c + 256, 256), bank[:, 0:256], 0.5, fv(halfb_c + c, 1),
                    ALU.mult, ALU.add),
                    waits=[(pe, b_fc2_last)] +
                          ([(ac, b_g_ac)] if (t + c) > 0 else []))
                fc2_evac[bank_i] = d_u
                dres = emit(dv, lambda h, bi=bi, c=c: h.tensor_add(
                    fv(t1_c + 256, 256), fv(t1_c + 256, 256),
                    bfv(xnb_c[bi] + 256 * c, 256)), waits=[(dv, d_u)])
                b_g_ac = emit(ac, lambda h, c=c: h.activation(
                    fv(gst_c + 256 * c, 256), fv(t1_c + 256, 256), AF.Silu),
                    waits=[(dv, dres)] +
                          ([(gpms, b_gst_done[t - 1])] if t >= 1 and c == 0 else []))
            dma(gp, g_s.ap()[t].rearrange("p c n -> p (c n)"), fv(gst_c, 2048),
                waits=[(ac, b_g_ac)])
            b_res_done[t] = dres
            b_gst_done[t] = flush(gp)

        barrier()
        _stop("B")

        dma(gp, bfv(maskbf_c, 128), mask_d.ap())
        wqkv_ms = []
        for jb in range(6):
            dma(gp, bfv(WA + jb * 4096, 4096).rearrange("p (c n) -> p c n", c=8),
                wqkv_d.ap()[:, jb * 512:(jb + 1) * 512].rearrange(
                    "(c p) n -> p c n", p=128))
            wqkv_ms.append(flush(gp))
        for kc in range(NCH):
            dma(gp, bfv(WA + 24576 + kc * 1024, 1024),
                wout_d.ap()[128 * kc:128 * (kc + 1), :])
        wout_ms = flush(gp)
        h1_c = WB + 0
        qb_c = WB + 4096
        kk_c = WB + 8192
        val_c = WB + 12800
        vsh_c = WB + 16896
        obs_c = WB + 20992
        pn_c = WB + 25088
        pnT_c = WB + 29184
        assert pnT_c + 2048 <= WB + 32768
        xn32t_c = PHF
        t1b_c = PHF + 4096
        xhst_c = [PHF + 4608, PHF + 5120]
        smx_c = PHF + 5632

        z1 = 0
        for c in range(NCH):
            z1 = emit(dv, lambda h, c=c: h.memset(bfv(kk_c + 576 * c, 64), 0.0))
        z2 = emit(dv, lambda h: h.memset(bfv(vsh_c, 1024, 0, 64), 0.0))

        c1_attn_pe = [0] * NC1
        prev_exps_bank = {}
        prev_ob_evac_last = 0
        prev_proj_evac = [0, 0]
        prev_t1 = 0
        c1_ln_done = [0] * NC1
        c1_xh_st = [0, 0]
        c1_qkv_pe = 0
        for t in range(min(NC1, c1_tiles)):
            for c in range(NCH):
                dma(sy, bfv(PHB + 512 * c, 512).rearrange(
                        "p (k n) -> p k n", k=4),
                    xnbf_s.ap()[4 * t:4 * t + 4, :, c, :].rearrange(
                        "k p n -> p k n"),
                    waits=([(dv, c1_ln_done[t - 1])] if c == 0 and t >= 1 else []))
            ldxms = flush(sy)
            for c in range(NCH):
                dma(sy, fv(xn32t_c + 512 * c, 512).rearrange(
                        "p (k n) -> p k n", k=4),
                    xn32_s.ap()[4 * t:4 * t + 4, :, c, :].rearrange(
                        "k p n -> p k n"),
                    waits=([(dv, prev_proj_evac[0]),
                            (dv, prev_proj_evac[1])] if c == 0 and t >= 1 else []))
            ld32ms = flush(sy)
            if t >= 1:
                cw = [(pe, c1_attn_pe[t - 1])]
                cr = 0
                for c in range(NCH):
                    cr = emit(dv, lambda h, c=c: h.tensor_copy(
                        bfv(kk_c + 576 * c, 64), bfv(kk_c + 576 * c + 512, 64)),
                        waits=(cw if c == 0 else []))
                cr = emit(dv, lambda h: h.tensor_copy(
                    bfv(vsh_c, 1024, 0, 64), bfv(val_c + 3 * 1024, 1024, 64, 128)),
                    waits=cw)
                carry_done = cr
            else:
                carry_done = z2
            r1w = [(dv, prev_t1)] if prev_t1 else []
            r1b = emit(pe, lambda h, t=t: h.matmul(
                pb[0][:, 0:512], fv(onesrow32_c, 128, 0, 1),
                fv(r1row_c + CT * t, 512, 0, 1), start=True, stop=True),
                waits=r1w)
            dvl = 0
            acl = 0
            acl_c = [0] * NCH
            for c in range(NCH):
                wdv = [(pe, r1b), (syms, ldxms)] if c == 0 else [(ac, acl)]
                dvl = emit(dv, lambda h, c=c: h.tensor_mul(
                    fv(t1b_c, 512), bfv(PHB + 512 * c, 512), pb[0][:, 0:512]),
                    waits=wdv)
                wac = [(dv, dvl)]
                if c == 0 and t >= 1:
                    wac.append((pe, c1_qkv_pe))
                acl = emit(ac, lambda h, c=c: h.activation(
                    bfv(h1_c + 512 * c, 512), fv(t1b_c, 512), AF.Identity,
                    bias=fv(pp_c + PPC["ln1_b"] + c, 1),
                    scale=fv(pp_c + PPC["ln1_g"] + c, 1)),
                    waits=wac)
                acl_c[c] = acl
            c1_ln_done[t] = dvl
            if c1_sub < 2:
                c1_attn_pe[t] = pe.count
                prev_t1 = c1_ln_done[t]
                continue
            vsub_evac = [0, 0]
            for s in range(4):
                bks = (pb[3], pb[4]) if s % 2 == 0 else (pb[5], pb[6])
                for dh in range(2):
                    bank = bks[dh]
                    par = (s % 2) * 2 + dh
                    for kc in range(NCH):
                        w6 = []
                        if s == 0 and dh == 0:
                            w6.append((ac, acl_c[kc]))
                        if kc == 0:
                            if t == 0:
                                w6.append((gpms, wqkv_ms[4 + dh]))
                            if s == 0 and dh == 0:
                                w6.append((dv, carry_done))
                            if s < 2:
                                if t == 0:
                                    if s % 2 == 0:
                                        w6.append((ac, p0_bvb))
                                elif prev_exps_bank.get(par):
                                    w6.append((ac, prev_exps_bank[par]))
                            if s >= 2 and vsub_evac[s % 2]:
                                w6.append((dv, vsub_evac[s % 2]))
                        vmm = emit(pe, lambda h, bank=bank, s=s, dh=dh, kc=kc:
                                   h.matmul(
                            bank[:, 0:512],
                            bfv(h1_c + 512 * kc + 128 * s, 128),
                            bfv(WA + (4 + dh) * 4096 + kc * 512, 512),
                            start=(kc == 0), stop=(kc == NCH - 1)), waits=w6,
                            inc=(1 if kc == NCH - 1 else 0))
                ev = 0
                for dh in range(2):
                    bank = bks[dh]
                    ev = emit(dv, lambda h, bank=bank, s=s, dh=dh: h.tensor_add(
                        bfv(val_c + s * 1024 + dh * 512, 512), bank[:, 0:512],
                        bfv(bvrow_bf_c + dh * 512, 512)),
                        waits=[(pe, vmm)] if dh == 0 else [])
                ev_val = ev
                for dh in range(2):
                    ev = emit(dv, lambda h, s=s, dh=dh: h.tensor_copy(
                        bfv(vsh_c + s * 1024 + dh * 512, 512, 64, 128),
                        bfv(val_c + s * 1024 + dh * 512, 512, 0, 64)),
                        waits=([(dv, ev_val)] if dh == 0 else []))
                    if s < 3:
                        ev = emit(dv, lambda h, s=s, dh=dh: h.tensor_copy(
                            bfv(vsh_c + (s + 1) * 1024 + dh * 512, 512, 0, 64),
                            bfv(val_c + s * 1024 + dh * 512, 512, 64, 128)))
                vsub_evac[s % 2] = ev
            v_done = vsub_evac[1]
            if c1_sub < 3:
                c1_attn_pe[t] = pe.count
                prev_t1 = c1_ln_done[t]
                continue
            groups = [(c, bg) for c in range(NCH) for bg in range(2)]
            smm_g = {}
            exps_bank = dict(prev_exps_bank)
            exp_g = {}
            ts_g = {}
            ptr_g = {}
            pT_g = {}
            omm_g = {}
            q_evac_c = [0] * NCH
            k_evac_c = [0] * NCH
            ob_evac = [0] * NCH

            def emit_q(c):
                for kc in range(NCH):
                    wq = []
                    if kc == 0:
                        if c == 0:
                            if t >= 1:
                                wq.append((dv, prev_proj_evac[0]))
                        else:
                            wq.append((ac, k_evac_c[c - 1]))
                        if t == 0:
                            wq.append((gpms, wqkv_ms[c // 4]))
                    qmm = emit(pe, lambda h, c=c, kc=kc: h.matmul(
                        pb[1][:, 0:512],
                        bfv(WA + (c // 4) * 4096 + kc * 512 + (c % 4) * 128, 128),
                        bfv(h1_c + 512 * kc, 512),
                        start=(kc == 0), stop=(kc == NCH - 1)), waits=wq,
                        inc=(1 if kc == NCH - 1 else 0))
                q_evac_c[c] = emit(ac, lambda h, c=c: h.activation(
                    bfv(qb_c + 512 * c, 512), pb[1][:, 0:512], AF.Identity,
                    bias=fv(bqs_c + c, 1), scale=0.125),
                    waits=[(pe, qmm)])

            def emit_k(c):
                for kc in range(NCH):
                    wk = []
                    if kc == 0:
                        wk.append((ac, q_evac_c[c]))
                        if t == 0:
                            wk.append((gpms, wqkv_ms[2 + c // 4]))
                    kmm = emit(pe, lambda h, c=c, kc=kc: h.matmul(
                        pb[1][:, 0:512],
                        bfv(WA + (2 + c // 4) * 4096 + kc * 512 + (c % 4) * 128, 128),
                        bfv(h1_c + 512 * kc, 512),
                        start=(kc == 0), stop=(kc == NCH - 1)), waits=wk,
                        inc=(1 if kc == NCH - 1 else 0))
                k_evac_c[c] = emit(ac, lambda h, c=c: h.activation(
                    bfv(kk_c + 576 * c + 64, 512), pb[1][:, 0:512], AF.Identity,
                    bias=fv(pp_c + PPC["b_qkv"] + 8 + c, 1), scale=1.0),
                    waits=[(pe, kmm)])

            def stage_a(g):
                c, bg = groups[g]
                par = g % 4
                par8 = g % 8
                sb = pb[3 + par]
                nmx = statc_c + 4 * par
                sm = rowtmp_c + 4 * par8
                smm = 0
                for b4 in range(4):
                    L = bg * 4 + b4
                    w7 = []
                    if b4 == 0:
                        w7.append((ac, k_evac_c[c]))
                        if g < 4:
                            w7.append((dv, v_done))
                        elif exps_bank.get(par):
                            w7.append((ac, exps_bank[par]))
                    for hh in range(2):
                        smm = emit(pe, lambda h, sb=sb, c=c, L=L, b4=b4, hh=hh:
                                   h.matmul(
                            sb[hh * 64:hh * 64 + 64, b4 * 128:b4 * 128 + 128],
                            bfv(qb_c + 512 * c + 64 * L, 64, hh * 64, hh * 64 + 64),
                            bfv(kk_c + 576 * c + 64 * L, 128, hh * 64, hh * 64 + 64),
                            start=True, stop=False),
                            waits=(w7 if hh == 0 else []), inc=0)
                        smm = emit(pe, lambda h, sb=sb, b4=b4, hh=hh: h.matmul(
                            sb[hh * 64:hh * 64 + 64, b4 * 128:b4 * 128 + 128],
                            bfv(idbf_c + hh * 64, 64, hh * 64, hh * 64 + 64),
                            bfv(maskbf_c, 128, hh * 64, hh * 64 + 64),
                            start=False, stop=True),
                            inc=(1 if (b4 == 3 and hh == 1) else 0))
                smm_g[g] = smm
                d_nm = emit(dv, lambda h, sb=sb, nmx=nmx: h.tensor_reduce(
                    fv(nmx, 4), sb[:, :].rearrange("p (g j) -> p g j", g=4),
                    AX.X, ALU.max, negate=True), waits=[(pe, smm)])
                a_e = 0
                for b4 in range(4):
                    w8 = []
                    if b4 == 0:
                        w8 = [(dv, d_nm)]
                        if ptr_g.get(g - 8):
                            w8.append((pe, ptr_g[g - 8]))
                        if ts_g.get(g - 8):
                            w8.append((dv, ts_g[g - 8]))
                    a_e = emit(ac, lambda h, b4=b4, par8=par8, nmx=nmx, sm=sm,
                               sb=sb: h.activation(
                        bfv(pn_c + 512 * par8 + 128 * b4, 128),
                        sb[:, 128 * b4:128 * b4 + 128],
                        AF.Exp, bias=fv(nmx + b4, 1), scale=1.0,
                        accum_out=fv(sm + b4, 1)), waits=w8)
                exp_g[g] = a_e
                exps_bank[par] = a_e

            def stage_b(g):
                c, bg = groups[g]
                par = g % 4
                par8 = g % 8
                sm = rowtmp_c + 4 * par8
                rs = statc_c + 32 + 4 * par
                d_rs = emit(dv, lambda h, sm=sm, rs=rs: h.reciprocal(
                    fv(rs, 4), fv(sm, 4)), waits=[(ac, exp_g[g])])
                d_ts = 0
                for b4 in range(4):
                    d_ts = emit(dv, lambda h, b4=b4, par8=par8, rs=rs:
                                h.tensor_scalar_mul(
                        bfv(pn_c + 512 * par8 + 128 * b4, 128),
                        bfv(pn_c + 512 * par8 + 128 * b4, 128),
                        fv(rs + b4, 1)),
                        waits=([(dv, d_rs)] if b4 == 0 else []))
                ts_g[g] = d_ts
                ptr = 0
                for b4 in range(4):
                    w9 = [(dv, d_ts)] if b4 == 0 else []
                    if b4 == 0 and pT_g.get(g - 2):
                        w9.append((dv, pT_g[g - 2]))
                    ptr = emit(pe, lambda h, par=par, par8=par8, b4=b4: h.transpose(
                        pbbf[:, (par % 2) * 512 + 128 * b4:
                             (par % 2) * 512 + 128 * b4 + 128],
                        bfv(pn_c + 512 * par8 + 128 * b4, 128), bfv(idbf_c, 128)),
                        waits=w9, inc=(1 if b4 == 3 else 0))
                ptr_g[g] = ptr
                wpT = [(pe, ptr)]
                if omm_g.get(g - 4):
                    wpT.append((pe, omm_g[g - 4]))
                d_pT = emit(dv, lambda h, par=par: h.tensor_copy(
                    bfv(pnT_c + 512 * par, 512),
                    pbbf[:, (par % 2) * 512:(par % 2) * 512 + 512]),
                    waits=wpT)
                pT_g[g] = d_pT
                if c1_sub == 41:
                    omm_g[g] = ptr
                    if bg == 1 and c >= 1:
                        ob_evac[c] = ob_evac[c - 1]
                    return
                omm = 0
                obank = pb[2]
                for b4 in range(4):
                    L = bg * 4 + b4
                    vcol = val_c + ((L - 1) // 2) * 1024 if L % 2 == 1 \
                        else vsh_c + (L // 2) * 1024
                    for hh in range(2):
                        w10 = [(dv, d_pT)] if (b4 == 0 and hh == 0) else []
                        if b4 == 0 and hh == 0 and bg == 0:
                            if c >= 1:
                                w10.append((ac, ob_evac[c - 1]))
                            elif t >= 1:
                                w10.append((dv, prev_proj_evac[1]))
                        omm = emit(pe, lambda h, vcol=vcol, c=c, obank=obank,
                                   L=L, hh=hh, par=par: h.matmul(
                            obank[hh * 64:hh * 64 + 64, L * 64:L * 64 + 64],
                            bfv(vcol + c * 128 + hh * 64, 64),
                            bfv(pnT_c + 512 * par + 128 * (L % 4) + hh * 64, 64),
                            start=True, stop=True), waits=w10,
                            inc=(1 if (b4 == 3 and hh == 1) else 0))
                omm_g[g] = omm
                if bg == 1:
                    ob_evac[c] = emit(ac, lambda h, c=c, obank=obank: h.activation(
                        bfv(obs_c + 512 * c, 512), obank[:, 0:512], AF.Identity),
                        waits=[(pe, omm)])

            gi = [0, 0]

            def pump_groups(max_a):
                while gi[0] < max_a:
                    n2 = min(2, max_a - gi[0])
                    for _ in range(n2):
                        stage_a(gi[0])
                        gi[0] += 1
                    if c1_sub == 40:
                        gi[1] = gi[0]
                        continue
                    while gi[1] <= gi[0] - 4 and gi[1] < 16:
                        stage_b(gi[1])
                        gi[1] += 1

            if c1_sub < 4:
                for c in range(NCH):
                    emit_q(c)
                    emit_k(c)
                c1_qkv_pe = pe.count
                c1_attn_pe[t] = pe.count
                prev_t1 = c1_ln_done[t]
                continue
            emit_q(0)
            emit_k(0)
            emit_q(1)
            emit_k(1)
            for c in range(2, NCH):
                emit_q(c)
                pump_groups(2 * (c - 1) - 1)
                emit_k(c)
                if c == NCH - 1:
                    c1_qkv_pe = pe.count
                pump_groups(2 * (c - 1))
            pump_groups(16)
            while gi[1] < 16:
                if c1_sub == 40:
                    gi[1] = 16
                    break
                stage_b(gi[1])
                gi[1] += 1
            c1_attn_pe[t] = pe.count
            if c1_sub < 5 or c1_sub in (40, 41, 42):
                prev_t1 = c1_ln_done[t]
                prev_exps_bank = dict(exps_bank)
                prev_ob_evac_last = ob_evac[NCH - 1]
                continue
            proj_evac = [0, 0]
            for co in range(NCH):
                bank_i = co % 2
                bank = pb[1 + bank_i]
                for kc in range(NCH):
                    w11 = []
                    if kc == 0:
                        w11 = [(ac, ob_evac[NCH - 1])] if co == 0 else []
                        if t == 0 and co == 0:
                            w11.append((gpms, wout_ms))
                        if proj_evac[bank_i]:
                            w11.append((dv, proj_evac[bank_i]))
                    pmm = emit(pe, lambda h, bank=bank, co=co, kc=kc: h.matmul(
                        bank[:, 0:512], bfv(WA + 24576 + kc * 1024 + co * 128, 128),
                        bfv(obs_c + 512 * kc, 512),
                        start=(kc == 0), stop=(kc == NCH - 1)), waits=w11,
                        inc=(1 if kc == NCH - 1 else 0))
                sbuf_i = co % 2
                w12 = [(pe, pmm), (syms, ld32ms)]
                if t >= 1 or co >= 2:
                    w12.append((gpms, c1_xh_st[sbuf_i]))
                d_pj = emit(dv, lambda h, bank=bank, co=co, sbuf_i=sbuf_i:
                            h.scalar_tensor_tensor(
                    fv(xhst_c[sbuf_i], 512), bank[:, 0:512],
                    fv(pp_c + PPC["b_out"] + co, 1), fv(xn32t_c + 512 * co, 512),
                    ALU.add, ALU.add), waits=w12)
                proj_evac[bank_i] = d_pj
                dma(gp, xh_s.ap()[co, :, CT * t:CT * (t + 1)],
                    fv(xhst_c[sbuf_i], 512), waits=[(dv, d_pj)])
                c1_xh_st[sbuf_i] = flush(gp)
            prev_proj_evac = [proj_evac[0], proj_evac[1]]
            prev_exps_bank = dict(exps_bank)
            prev_ob_evac_last = ob_evac[NCH - 1]
            prev_t1 = c1_ln_done[t]

        c1_sy_f = flush(sy)
        c1_gp_f = flush(gp)
        c1_pre = [(pe, pe.count), (dv, dv.count), (ac, ac.count),
                  (syms, c1_sy_f), (gpms, c1_gp_f)]
        c1_pe_end = pe.count

        w1_ms = []
        for jb in range(8):
            wv = c1_qkv_pe if jb == 0 else (c1_pe_end if jb == 6 else 0)
            dma(gp, bfv(WA + jb * 4096, 4096).rearrange("p (c n) -> p c n", c=8),
                w1_d.ap()[:, jb * 512:(jb + 1) * 512].rearrange(
                    "(c p) n -> p c n", p=128),
                waits=([(pe, wv)] if wv else []))
            w1_ms.append(flush(gp))
        for g8 in range(8):
            dma(gp, bfv(WB + 4 * g8 * 1024, 4096).rearrange("p (b n) -> p b n", b=4),
                w2_d.ap()[512 * g8:512 * (g8 + 1), :].rearrange(
                    "(b p) n -> p b n", p=128))
        w2_ms = flush(gp)
        for e in ALLE:
            ws = [(o, c) for (o, c) in c1_pre if o is not e and c > 0]
            emit(e, lambda h: None, waits=ws, inc=0)
        _stop("C1")
        xhb_c = PHB
        sqb_c = PHB + 2048
        h2_c = PHB + 4096
        hid2_c = PHB + 6144
        xht_c = PHF
        xtm2_c = PHF + 2048
        ssc_c = PHF + 4096
        gt32_c = PHF + 4352
        sg32_c = PHF + 5376
        assert sg32_c + 1024 <= PHF + 6656

        def gtb(col, n):
            return fv(gt32_c + col // 2, n // 2).bitcast(BF16)

        def sgb(col, n):
            return fv(sg32_c + col // 2, n // 2).bitcast(BF16)

        rw_c = [rowtmp_c, r1row_c]
        r2r_c = [r2row_c, r1row_c + 512]
        rh_c = [rhs2_c, r1row_c + 768]
        r2bf_p = [r2bf_c, r2bf2_c]
        rhs2bf_p = [rhs2bf_c, rhs2bf2_c]

        c2_ldh = [0] * NB
        c2_ldg = [0] * NB
        c2_ldx = [0] * NB
        c2_sg_done = [0] * NB
        c2_store = [0] * NB
        c2_stats_dv = [0] * NB
        c2_sq_smm = [0] * NB
        c2_asq = [0] * NB
        c2_h2_done = [0] * NB
        c2_w1_last = [0] * NB
        c2_tail_pe = [0] * NB
        c2_bmm = [0] * NB
        prev_ab_dv = [0, 0]
        prev_hid2_ac = [0, 0]
        prev_w2_dv = [0, 0]

        def c2_loads(t):
            for c in range(NCH):
                dma(sy, fv(xht_c + 256 * c, 256),
                    xh_s.ap()[c, :, BT * t:BT * (t + 1)],
                    waits=([(dv, c2_sg_done[t - 1])] if c == 0 and t >= 1 else []))
            c2_ldh[t] = flush(sy)
            for c in range(NCH):
                dma(sy, gtb(256 * c, 256), g_s.ap()[t, :, c, :],
                    waits=([(dv, c2_sg_done[t - 1])] if c == 0 and t >= 1 else []))
            c2_ldg[t] = flush(sy)
            for s2 in range(2):
                dma(sy, fv(xtm2_c + 1024 * s2, 1024),
                    x_d.ap()[BT * t + 128 * s2:BT * t + 128 * (s2 + 1), :],
                    waits=([(gpms, c2_store[t - 1])] if s2 == 0 and t >= 1 else []))
            c2_ldx[t] = flush(sy)

        def c2_stats(t):
            p = t % 2
            w = [(syms, c2_ldh[t])]
            if t >= 1:
                w += [(pe, c2_sq_smm[t - 1]), (ac, c2_asq[t - 1])]
            d_bf = emit(dv, lambda h: h.tensor_copy(bfv(xhb_c, 2048),
                                                    fv(xht_c, 2048)), waits=w)
            a_sq = emit(ac, lambda h: h.activation(bfv(sqb_c, 2048),
                                                   bfv(xhb_c, 2048), AF.Square),
                        waits=[(dv, d_bf)])
            c2_asq[t] = a_sq
            smm = 0
            for c in range(NCH):
                smm = emit(pe, lambda h, c=c: h.matmul(
                    pb[0][0:1, 0:256], bfv(onescol_bf_c, 1),
                    bfv(xhb_c + 256 * c, 256),
                    start=(c == 0), stop=(c == NCH - 1)),
                    waits=([(dv, d_bf)] +
                           ([(dv, c2_stats_dv[t - 1])] if t >= 1 else []))
                    if c == 0 else [], inc=(1 if c == NCH - 1 else 0))
            for c in range(NCH):
                smm = emit(pe, lambda h, c=c: h.matmul(
                    pb[0][0:1, 256:512], bfv(onescol_bf_c, 1),
                    bfv(sqb_c + 256 * c, 256),
                    start=(c == 0), stop=(c == NCH - 1)),
                    waits=([(ac, a_sq)] if c == 0 else []),
                    inc=(1 if c == NCH - 1 else 0))
            c2_sq_smm[t] = smm
            rw = rw_c[p]
            r2r = r2r_c[p]
            rh = rh_c[p]
            wrm = [(pe, smm)]
            if t >= 2:
                wrm.append((pe, c2_bmm[t - 2]))
            d_p2 = emit(dv, lambda h, rw=rw: h.tensor_scalar_mul(
                fv(rw, 256, 0, 1), pb[0][0:1, 0:256], 1.0 / C), waits=wrm)
            d_p2 = emit(dv, lambda h, rw=rw: h.tensor_scalar_mul(
                fv(rw + 256, 256, 0, 1), pb[0][0:1, 256:512], 1.0 / C),
                waits=[(dv, d_p2)])
            d_p2 = emit(dv, lambda h, rw=rw, r2r=r2r: h.tensor_mul(
                fv(r2r, 256, 0, 1), fv(rw, 256, 0, 1), fv(rw, 256, 0, 1)),
                waits=[(dv, d_p2)])
            d_v2 = emit(dv, lambda h, rw=rw, r2r=r2r: h.tensor_sub(
                fv(rw + 256, 256, 0, 1), fv(rw + 256, 256, 0, 1),
                fv(r2r, 256, 0, 1)), waits=[(dv, d_p2)])
            c2_stats_dv[t] = d_v2
            a_sd = emit(ac, lambda h, rw=rw: h.activation(
                fv(rw + 256, 256, 0, 1), fv(rw + 256, 256, 0, 1), AF.Sqrt,
                bias=fv(statc_c + 63, 1, 0, 1), scale=1.0), waits=[(dv, d_v2)])
            d_r2 = emit(dv, lambda h, rw=rw, r2r=r2r: h.reciprocal(
                fv(r2r, 256, 0, 1), fv(rw + 256, 256, 0, 1)), waits=[(ac, a_sd)])
            d_ms = emit(dv, lambda h, rh=rh: h.memset(fv(rh, 256, 0, 2), -1.0),
                        waits=[(dv, d_r2)])
            d_rh = emit(dv, lambda h, rw=rw, r2r=r2r, rh=rh: h.tensor_mul(
                fv(rh, 256, 0, 1), fv(rw, 256, 0, 1), fv(r2r, 256, 0, 1)),
                waits=[(dv, d_ms)])
            a_r2b = emit(ac, lambda h, p=p, r2r=r2r: h.copy(
                bfv(r2bf_p[p], 256, 0, 1), fv(r2r, 256, 0, 1)),
                waits=[(dv, d_r2)])
            a_rhb = emit(ac, lambda h, p=p, rh=rh: h.copy(
                bfv(rhs2bf_p[p], 256, 0, 2), fv(rh, 256, 0, 2)),
                waits=[(dv, d_rh)])
            d_h2 = 0
            bmm = 0
            for c in range(NCH):
                bank_i = c % 2
                bank = pb[1 + bank_i]
                w14 = [(ac, a_r2b), (ac, a_rhb)] if c == 0 else []
                if c == 0 and t >= 1:
                    w14.append((pe, c2_w1_last[t - 1]))
                if prev_ab_dv[bank_i]:
                    w14.append((dv, prev_ab_dv[bank_i]))
                bmm = emit(pe, lambda h, bank=bank, c=c, p=p: h.matmul(
                    bank[:, 0:256], bfv(gb2bf_c + 128 * c, 128, 0, 1),
                    bfv(r2bf_p[p], 256, 0, 1), start=True, stop=True), waits=w14)
                bmm = emit(pe, lambda h, bank=bank, c=c, p=p: h.matmul(
                    bank[:, 256:512], bfv(gb2bf_c + 128 * c, 128, 0, 2),
                    bfv(rhs2bf_p[p], 256, 0, 2), start=True, stop=True))
                d_tmp = emit(dv, lambda h, bank=bank, c=c: h.tensor_mul(
                    fv(ssc_c, 256), fv(xht_c + 256 * c, 256), bank[:, 0:256]),
                    waits=[(pe, bmm)])
                d_h2 = emit(dv, lambda h, bank=bank, c=c: h.tensor_sub(
                    bfv(h2_c + 256 * c, 256), fv(ssc_c, 256), bank[:, 256:512]),
                    waits=[(dv, d_tmp)])
                prev_ab_dv[bank_i] = d_h2
            c2_bmm[t] = bmm
            c2_h2_done[t] = d_h2

        c2_loads(0)
        c2_stats(0)
        for t in range(NB):
            mm1 = 0
            for pr in range(16):
                bank_i = pr % 2
                bank = pb[3 + bank_i]
                for half in range(2):
                    ngc = pr * 2 + half
                    for kc in range(NCH):
                        w15 = []
                        if kc == 0 and half == 0:
                            w15 = [(dv, c2_h2_done[t])] if pr == 0 else []
                            if t == 0:
                                w15.append((gpms, w1_ms[pr // 2]))
                            if prev_hid2_ac[bank_i]:
                                w15.append((ac, prev_hid2_ac[bank_i]))
                        mm1 = emit(pe, lambda h, bank=bank, half=half, ngc=ngc,
                                   kc=kc: h.matmul(
                            bank[:, half * 256:half * 256 + 256],
                            bfv(WA + (ngc // 4) * 4096 + kc * 512 + (ngc % 4) * 128,
                                128),
                            bfv(h2_c + 256 * kc, 256),
                            start=(kc == 0), stop=(kc == NCH - 1)), waits=w15,
                            inc=(1 if (kc == NCH - 1 and half == 1) else 0))
                ev = 0
                for half in range(2):
                    ngc = pr * 2 + half
                    ev = emit(ac, lambda h, bank=bank, half=half, ngc=ngc:
                              h.activation(
                        bfv(hid2_c + 256 * ngc, 256),
                        bank[:, half * 256:half * 256 + 256], AF.Gelu,
                        bias=fv(pp_c + PPC["b1"] + ngc, 1), scale=1.0),
                        waits=[(pe, mm1)] if half == 0 else [])
                prev_hid2_ac[bank_i] = ev
            c2_w1_last[t] = pe.count
            d_sg = 0
            for c in range(NCH):
                bank_i = c % 2
                bank = pb[5 + bank_i]
                for kc in range(F // 128):
                    w16 = []
                    if kc == 0:
                        w16 = [(ac, prev_hid2_ac[0]), (ac, prev_hid2_ac[1])] \
                            if c == 0 else []
                        if t == 0 and c == 0:
                            w16.append((gpms, w2_ms))
                        if prev_w2_dv[bank_i]:
                            w16.append((dv, prev_w2_dv[bank_i]))
                    mm2 = emit(pe, lambda h, bank=bank, c=c, kc=kc: h.matmul(
                        bank[:, 0:256], bfv(WB + kc * 1024 + c * 128, 128),
                        bfv(hid2_c + 256 * kc, 256),
                        start=(kc == 0), stop=(kc == F // 128 - 1)), waits=w16,
                        inc=(1 if kc == F // 128 - 1 else 0))
                d_s = emit(dv, lambda h, bank=bank, c=c: h.scalar_tensor_tensor(
                    fv(ssc_c, 256), bank[:, 0:256], fv(pp_c + PPC["b2"] + c, 1),
                    fv(xht_c + 256 * c, 256), ALU.add, ALU.add),
                    waits=[(pe, mm2)])
                prev_w2_dv[bank_i] = d_s
                wsg = [(dv, d_s)]
                if c == 0:
                    wsg.append((syms, c2_ldg[t]))
                    if t >= 1:
                        wsg.append((pe, c2_tail_pe[t - 1]))
                d_sg = emit(dv, lambda h, c=c: h.tensor_mul(
                    sgb(256 * c, 256), fv(ssc_c, 256), gtb(256 * c, 256)),
                    waits=wsg)
            c2_sg_done[t] = d_sg
            if t + 1 < NB:
                c2_loads(t + 1)
            dvo = 0
            for s2 in range(2):
                ptr = 0
                for c in range(NCH):
                    w17 = []
                    if c == 0:
                        w17 = [(dv, d_sg)]
                        if t >= 1 or s2 == 1:
                            w17.append((dv, dvo))
                    ptr = emit(pe, lambda h, c=c, s2=s2: h.transpose(
                        pbbf[:, c * 128:c * 128 + 128],
                        sgb(256 * c + 128 * s2, 128), bfv(idbf_c, 128)),
                        waits=w17, inc=(1 if c == NCH - 1 else 0))
                for c in range(NCH):
                    w18 = [(pe, ptr), (syms, c2_ldx[t])] if c == 0 else []
                    dvo = emit(dv, lambda h, c=c, s2=s2: h.tensor_add(
                        fv(xtm2_c + 1024 * s2 + 128 * c, 128),
                        fv(xtm2_c + 1024 * s2 + 128 * c, 128),
                        pbbf[:, c * 128:c * 128 + 128]), waits=w18)
                dma(gp, y_d.ap()[BT * t + 128 * s2:BT * t + 128 * (s2 + 1), :],
                    fv(xtm2_c + 1024 * s2, 1024), waits=[(dv, dvo)])
            c2_tail_pe[t] = pe.count
            c2_store[t] = flush(gp)
            if t + 1 < NB:
                c2_stats(t + 1)

        flush(sy)
        flush(gp)

        with nc.Block() as block:
            @block.sync
            def _(h):
                for fn in sy.stream:
                    fn(h)

            @block.gpsimd
            def _(h):
                for fn in gp.stream:
                    fn(h)

            @block.tensor
            def _(h):
                for fn in pe.stream:
                    fn(h)

            @block.vector
            def _(h):
                for fn in dv.stream:
                    fn(h)

            @block.scalar
            def _(h):
                for fn in ac.stream:
                    fn(h)

    return nc


def _pack_params(ins):
    cols = []
    for nm, key in [("ln1_g", "L0_attn_ln1_g"), ("ln1_b", "L0_attn_ln1_b"),
                    ("ln2_g", "L0_attn_ln2_g"), ("ln2_b", "L0_attn_ln2_b"),
                    ("lnf_g", "L0_ffn_ln_gamma"), ("lnf_b", "L0_ffn_ln_beta"),
                    ("b_qkv", "L0_attn_b_qkv"), ("b_out", "L0_attn_b_out"),
                    ("b1", "L0_attn_b1"), ("b2", "L0_attn_b2"),
                    ("fc1b", "L0_ffn_fc1_bias"), ("fc2b", "L0_ffn_fc2_bias")]:
        v = np.asarray(ins[key], dtype=np.float32)
        cols.append(v.reshape(-1, 128).T)
    return np.ascontiguousarray(np.concatenate(cols, axis=1))


def _mask128():
    i = np.arange(64)
    j = np.arange(128)
    allowed = (j[None, :] < 64) | ((j[None, :] - 64) <= i[:, None])
    m64 = np.where(allowed, 0.0, NEG).astype(np.float32)
    return np.ascontiguousarray(np.concatenate([m64, m64], axis=0))


def kernel(**ins):
    x = np.ascontiguousarray(np.asarray(ins["L0_attn_x"], dtype=np.float32))
    shared = {
        "wqkv": np.ascontiguousarray(np.asarray(ins["L0_attn_w_qkv"], np.float32)),
        "wout": np.ascontiguousarray(np.asarray(ins["L0_attn_w_out"], np.float32)),
        "w1": np.ascontiguousarray(np.asarray(ins["L0_attn_w1"], np.float32)),
        "w2": np.ascontiguousarray(np.asarray(ins["L0_attn_w2"], np.float32)),
        "fc1q": np.ascontiguousarray(np.asarray(ins["L0_ffn_fc1_q"], np.int32)),
        "fc2q": np.ascontiguousarray(np.asarray(ins["L0_ffn_fc2_q"], np.int32)),
        "fc1s": np.ascontiguousarray(np.asarray(ins["L0_ffn_fc1_scales"], np.float32)),
        "fc2s": np.ascontiguousarray(np.asarray(ins["L0_ffn_fc2_scales"], np.float32)),
        "pp": _pack_params(ins),
        "gb2": np.ascontiguousarray(np.stack([
            np.asarray(ins["L0_attn_ln2_g"], np.float32),
            np.asarray(ins["L0_attn_ln2_b"], np.float32)])),
        "bv": np.ascontiguousarray(
            np.asarray(ins["L0_attn_b_qkv"], np.float32)[2048:3072][None, :]),
        "ident": np.eye(128, dtype=np.float32),
        "mask": _mask128(),
    }
    nc = build_nc()
    in_maps = []
    for core in range(8):
        m = dict(shared)
        m["x"] = x[core]
        in_maps.append(m)
    res = run_bass_kernel_spmd(nc, in_maps, core_ids=list(range(8)))
    out = np.stack([np.asarray(r["y"], np.float32) for r in res.results], axis=0)
    return out


if __name__ == "__main__":
    import reference as _r
    ins = _r.setup_inputs()
    out = kernel(**{k: np.asarray(v) for k, v in ins.items()})
    ref = np.asarray(_r.reference(**ins))
    err = np.abs(out - ref)
    denom = np.square(ref).mean()
    print("resvar:", float(np.square(out - ref).mean() / denom),
          "maxabs:", float(err.max()))
